# version 12
# baseline (speedup 1.0000x reference)
import numpy as np
import concourse.bass as bass
import concourse.mybir as mybir
from concourse.bass_utils import run_bass_kernel_spmd

F32 = mybir.dt.float32
BF16 = mybir.dt.bfloat16
AF = mybir.ActivationFunctionType
ALU = mybir.AluOpType
AX = mybir.AxisListType

D = 2048
NKC = 16
H = 16
S = 2048
NQ = 1024
NQC = 8
NTC = 16
NHC = 64
EPS = 1e-5
QK = 1.0 / float(np.float32(np.sqrt(128.0)))
SH = 3.0

LN1 = 0
QF = 32768
PTR = 49152
KSB = 53248
VFM = 57344
VTM = 59392
W3 = 61440
WDA = 67584
IDENT = 69632
ONESC = 69760
ONESR = 69888
RB = 70016
RECIP = 71040
OFM = 72064
TAIL = 88448
ABF_N = 92544

XS0 = 0
XS1 = 2048
T1 = 4096
ST = 6144
AF32_N = 6656

GMAP = {}
for _hc in range(32):
    GMAP[_hc] = 1024 * _hc
for _hc in range(32, 48):
    GMAP[_hc] = QF + 1024 * (_hc - 32)
GMAP[48] = 51200
GMAP[49] = 52224
for _hc in range(50, 54):
    GMAP[_hc] = KSB + 1024 * (_hc - 50)
GMAP[54] = VFM
GMAP[55] = VFM + 1024
GMAP[56] = VTM
GMAP[57] = VTM + 1024
GMAP[58] = RB
GMAP[59] = RECIP
for _hc in range(60, 64):
    GMAP[_hc] = TAIL + 1024 * (_hc - 60)
WDB = PTR


_STOP_AFTER = "G"


class Ctr:
    __slots__ = ("s", "n")

    def __init__(self, s):
        self.s = s
        self.n = 0


def build_nc():
    nc = bass.Bass("TRN2", target_bir_lowering=False)
    xb = nc.dram_tensor("xb", [S, D], F32, kind="ExternalInput").ap()
    wq = nc.dram_tensor("wq", [H, 128, D], F32, kind="ExternalInput").ap()
    wk = nc.dram_tensor("wk", [H, 128, D], F32, kind="ExternalInput").ap()
    wv = nc.dram_tensor("wv", [H, 128, D], F32, kind="ExternalInput").ap()
    wo = nc.dram_tensor("wo", [NKC, 128, D], F32, kind="ExternalInput").ap()
    wg = nc.dram_tensor("wg", [NHC, 128, D], F32, kind="ExternalInput").ap()
    wu = nc.dram_tensor("wu", [NHC, 128, D], F32, kind="ExternalInput").ap()
    wd = nc.dram_tensor("wd", [4, NHC, 128, 512], F32, kind="ExternalInput").ap()
    idn = nc.dram_tensor("idn", [128, 128], F32, kind="ExternalInput").ap()
    onc = nc.dram_tensor("onc", [128, 128], F32, kind="ExternalInput").ap()
    onr = nc.dram_tensor("onr", [1, 128], F32, kind="ExternalInput").ap()
    cst = nc.dram_tensor("cst", [128, 2], F32, kind="ExternalInput").ap()
    x2s = nc.dram_tensor("x2s", [NQ, D], F32)
    out = nc.dram_tensor("out", [NQ, D], F32, kind="ExternalOutput").ap()
    x2sa = x2s.ap()

    import contextlib
    es = contextlib.ExitStack()
    ABF = es.enter_context(nc.sbuf_tensor("abf", [128, ABF_N], BF16))
    AFP = es.enter_context(nc.sbuf_tensor("afp", [128, AF32_N], F32))
    PS = es.enter_context(nc.psum_tensor("ps", [128, 4096], F32))

    def Bv(off, dims, parts=128):
        return bass.AP(ABF, off, [[ABF_N, parts]] + [list(d) for d in dims])

    def Fv(off, dims, parts=128):
        return bass.AP(AFP, off, [[AF32_N, parts]] + [list(d) for d in dims])

    def Pv(off, dims, parts=128):
        return bass.AP(PS, off, [[4096, parts]] + [list(d) for d in dims])

    def stat(par, k):
        return Fv(ST + par * 8 + k, [(1, 1)])

    sems = {}

    def C(name):
        c = Ctr(es.enter_context(nc.semaphore(name)))
        sems[name] = c
        return c

    dM = C("dM")
    dW = [C("dW0"), C("dW1"), C("dW2")]
    dWO = [C("dWO0"), C("dWO1"), C("dWO2"), C("dWO3")]
    dWD = [C("dWDa"), C("dWDb")]
    dOC = C("dOC")
    dAC = [C("dAC0"), C("dAC1"), C("dAC2")]
    dXA = [C("dXA0"), C("dXA1"), C("dXA2")]
    dXO = [C("dXO0"), C("dXO1")]
    dSP = [C("dSP0"), C("dSP1")]
    tTR = C("tTR")
    tQ = C("tQ")
    tKV = C("tKV")
    tVt = C("tVt")
    tS = C("tS")
    tRd = C("tRd")
    tRB = C("tRB")
    tD = C("tD")
    tT3 = C("tT3")
    tE = C("tE")
    tF = C("tF")
    sI = C("sI")
    sSq = C("sSq")
    sSd = C("sSd")
    sQe = C("sQe")
    sKV = C("sKV")
    sE = C("sE")
    sRB = C("sRB")
    sSm = C("sSm")
    sSil = C("sSil")
    sFe = C("sFe")
    vI = C("vI")
    vXA = C("vXA")
    vA = C("vA")
    vVt = C("vVt")
    vRc = C("vRc")
    vO = C("vO")
    vD = C("vD")
    vA3 = C("vA3")
    vGa = C("vGa")
    vFe = C("vFe")
    vG = C("vG")
    sFs = C("sFs")
    vFs = C("vFs")

    ops = {"t": [], "v": [], "s": [], "g": [], "y": []}
    marks = {}

    def op(e, f, ride=None):
        pair = None
        if ride is not None:
            c, k = ride
            c.n += k
            pair = (c.s, k)
        if pair is None:
            ops[e].append(f)
        else:
            def g(eng, f=f, pair=pair):
                f(eng).then_inc(*pair)
            ops[e].append(g)

    def w(e, c, v=None):
        if v is None:
            c, v = c
        ops[e].append(lambda eng, s=c.s, v=v: eng.wait_ge(s, v))

    site_par = [0]

    def ln_site(src_v, src_s, dump, vwaits, swaits, psum_src=False):
        par = site_par[0] % 4
        site_par[0] += 1
        SUM, M, M2, VE, SD, RS, SQ = range(7)
        for c, v in vwaits:
            w("v", c, v)
        op("v", lambda e: e.reduce_sum(stat(par, SUM), src_v(), axis=AX.X), (vI, 1))
        for c, v in swaits:
            w("s", c, v)
        if psum_src:
            w("s", vI, vI.n)
        op("s", lambda e: e.activation(dump(), src_s(), AF.Square,
                                       accum_out=stat(par, SQ)), (sSq, 1))
        w("v", vI, vI.n)
        op("v", lambda e: e.tensor_scalar(stat(par, M), stat(par, SUM),
                                          1.0 / D, None, ALU.mult), (vI, 1))
        w("v", vI, vI.n)
        op("v", lambda e: e.tensor_scalar(stat(par, M2), stat(par, M),
                                          stat(par, M), None, ALU.mult), (vI, 1))
        w("v", sSq, sSq.n)
        w("v", vI, vI.n)
        op("v", lambda e: e.scalar_tensor_tensor(stat(par, VE), stat(par, SQ),
                                                 1.0 / D, stat(par, M2),
                                                 ALU.mult, ALU.subtract), (vI, 1))
        w("s", vI, vI.n)
        op("s", lambda e: e.activation(stat(par, SD), stat(par, VE), AF.Sqrt,
                                       bias=Fv(ST + 40, [(1, 1)])), (sSd, 1))
        w("v", sSd, sSd.n)
        op("v", lambda e: e.reciprocal(stat(par, RS), stat(par, SD)), (vI, 1))
        return par, M, RS

    op("g", lambda e: e.dma_start(Bv(IDENT, [(1, 128)]), idn[:, :]), (dM, 16))
    op("g", lambda e: e.dma_start(Bv(ONESC, [(1, 128)]), onc[:, :]), (dM, 16))
    op("g", lambda e: e.dma_start(Bv(ONESR, [(1, 128)], parts=1), onr[:, :]), (dM, 16))
    op("g", lambda e: e.dma_start(Fv(ST + 40, [(1, 2)]), cst[:, :],
                                  allow_slow_non_contiguous=True), (dM, 16))
    w("s", dM, 64)

    def a_stage1(tc):
        slot = tc % 3
        if tc >= 3:
            w("y", vXA, tc - 2)
        op("y", lambda e, slot=slot, tc=tc: e.dma_start(
            Fv(XS0 + slot * 2048, [(1, 2048)]),
            xb[tc * 128:(tc + 1) * 128, :]), (dXA[slot], 16))
        xs_v = (lambda slot=slot: Fv(XS0 + slot * 2048, [(1, 2048)]))
        par, M, RS = ln_site(
            src_v=xs_v, src_s=xs_v,
            dump=lambda: Bv(KSB, [(1, 2048)]),
            vwaits=[(dXA[slot], 16 * (tc // 3 + 1))],
            swaits=[(dXA[slot], 16 * (tc // 3 + 1))])
        w("v", vI, vI.n)
        op("v", lambda e, par=par, M=M, RS=RS: e.tensor_scalar(
            stat(par, 7), stat(par, M),
            stat(par, RS), -1.0, ALU.mult, ALU.mult), (vI, 1))
        w("s", vI, vI.n)
        if tc >= 2:
            w("s", tTR, tc - 1)
        op("s", lambda e, slot=slot, par=par, RS=RS: e.activation(
            Bv(OFM + (tc % 2) * 2048, [(1, 2048)]),
            Fv(XS0 + slot * 2048, [(1, 2048)]),
            AF.Identity, bias=stat(par, 7), scale=stat(par, RS)), (vXA, 1))

    def a_stage2(tc):
        slot = tc % 2
        if tc == 0:
            w("t", dM, 64)
        w("t", vXA, tc + 1)
        if tc >= 1:
            w("t", vA, tc)
        for kc in range(NKC):
            ride = (tTR, 1) if kc == NKC - 1 else None
            op("t", lambda e, slot=slot, kc=kc: e.matmul(
                Pv(kc * 128, [(1, 128)]),
                Bv(OFM + slot * 2048 + kc * 128, [(1, 128)]),
                Bv(IDENT, [(1, 128)]),
                start=True, stop=True), ride)
        w("v", tTR, tc + 1)
        op("v", lambda e, tc=tc: e.tensor_copy(
            Bv(LN1 + tc * 128, [(2048, 16), (1, 128)]),
            Pv(0, [(128, 16), (1, 128)])), (vA, 1))

    a_stage1(0)
    a_stage1(1)
    for tc in range(2, 8):
        a_stage1(tc)
        a_stage2(tc - 2)
    a_stage2(6)
    a_stage2(7)

    marks["A"] = {e: len(l) for e, l in ops.items()}
    consumed_at = {}
    wslot_cnt = [0, 0, 0]

    def wdma(src_ap, item, cons_wait):
        slot = item % 3
        if cons_wait is not None:
            w("g", cons_wait[0], cons_wait[1])
        wslot_cnt[slot] += 1
        op("g", lambda e, slot=slot, src_ap=src_ap: e.dma_start(
            Bv(W3 + slot * 2048, [(1, 2048)]), src_ap), (dW[slot], 16))
        return slot, 16 * wslot_cnt[slot]

    for h in range(H):
        if h < 8:
            a_stage1(8 + h)
        if 2 <= h < 10:
            a_stage2(6 + h)
        cons = (tQ, h - 2) if h >= 3 else None
        slot, dwv = wdma(wq[h, :, :], h, cons)
        if h == 0:
            w("t", vA, 8)
        w("t", dW[slot], dwv)
        if h >= 2:
            w("t", sQe, h - 1)
        pair = h % 2
        for kc in range(NKC):
            for half in range(2):
                ride = (tQ, 1) if (kc == NKC - 1 and half == 1) else None
                op("t", lambda e, slot=slot, kc=kc, half=half, pair=pair: e.matmul(
                    Pv(2048 + pair * 1024 + half * 512, [(1, 512)]),
                    Bv(W3 + slot * 2048 + kc * 128, [(1, 128)]),
                    Bv(LN1 + kc * 2048 + half * 512, [(1, 512)]),
                    start=(kc == 0), stop=(kc == NKC - 1)), ride)
        consumed_at[h] = (tQ, tQ.n)
        w("s", tQ, h + 1)
        op("s", lambda e, h=h, pair=pair: e.activation(
            Bv(QF + h * 1024, [(1, 1024)]),
            Pv(2048 + pair * 1024, [(1, 1024)]), AF.Copy), (sQe, 1))

    marks["B"] = {e: len(l) for e, l in ops.items()}
    kv_evac = {}
    kv_mm = {}
    vt_mm = {}
    wk_slot = {}
    wv_slot = {}
    vsm = C("vSm")

    def emit_wkv(h):
        m_k = 16 + 2 * h
        cons_k = consumed_at.get(m_k - 3)
        wk_slot[h] = wdma(wk[h, :, :], m_k, cons_k)
        m_v = 17 + 2 * h
        cons_v = consumed_at.get(m_v - 3)
        wv_slot[h] = wdma(wv[h, :, :], m_v, cons_v)

    def emit_kv_group(kind, h, g):
        slot, dwv = wk_slot[h] if kind == "K" else wv_slot[h]
        if g == 0:
            w("t", dW[slot], dwv)
        if kind == "K":
            if h == 0:
                w("t", vA, 4 * (g + 1))
            if h == 0 and g < 2:
                w("t", sQe, 16)
            elif g < 2 and h >= 1:
                w("t", kv_evac[("V", h - 1, g + 2)])
            elif g >= 2:
                w("t", kv_evac[("K", h, g - 2)])
            if h >= 2 and g < 2:
                w("t", vRc, h - 1)
        else:
            w("t", kv_evac[("K", h, g + 2)] if g < 2 else kv_evac[("V", h, g - 2)])
        bank = 4 + (g % 2)
        for kc in range(NKC):
            ride = (tKV, 1) if kc == NKC - 1 else None
            op("t", lambda e, slot=slot, kc=kc, g=g, bank=bank: e.matmul(
                Pv(bank * 512, [(1, 512)]),
                Bv(W3 + slot * 2048 + kc * 128, [(1, 128)]),
                Bv(LN1 + kc * 2048 + g * 512, [(1, 512)]),
                start=(kc == 0), stop=(kc == NKC - 1)), ride)
        kv_mm[(kind, h, g)] = (tKV, tKV.n)
        if kind == "K" and g == 3:
            consumed_at[16 + 2 * h] = (tKV, tKV.n)
        if kind == "V" and g == 3:
            consumed_at[17 + 2 * h] = (tKV, tKV.n)

    def emit_kv_evac(kind, h, g):
        w("s", kv_mm[(kind, h, g)])
        if kind == "V" and g == 0 and h >= 1:
            w("s", vt_mm[(h - 1, 1)])
        bank = 4 + (g % 2)
        dst = (KSB + (h % 2) * 2048) if kind == "K" else VFM
        op("s", lambda e, g=g, dst=dst, bank=bank: e.activation(
            Bv(dst + g * 512, [(1, 512)]),
            Pv(bank * 512, [(1, 512)]), AF.Copy), (sKV, 1))
        kv_evac[(kind, h, g)] = (sKV, sKV.n)

    def emit_vt(h):
        for r in range(2):
            w("t", kv_evac[("V", h, 2 * r + 1)])
            if r == 0 and h == 0:
                w("t", vA, 16)
            if r == 0 and h >= 1:
                w("t", sE, 16 * h)
            for i in range(8):
                tcv = r * 8 + i
                ride = (tVt, 1) if i == 7 else None
                op("t", lambda e, tcv=tcv, r=r, i=i: e.matmul(
                    Pv(r * 1024 + i * 128, [(1, 128)]),
                    Bv(VFM + tcv * 128, [(1, 128)]),
                    Bv(IDENT, [(1, 128)]),
                    start=True, stop=True), ride)
            vt_mm[(h, r)] = (tVt, tVt.n)

    def emit_vt_copies(h):
        for r in range(2):
            w("v", vt_mm[(h, r)])
            if h >= 1 and r == 0:
                w("v", tRd, 4 * h)
            op("v", lambda e, r=r: e.tensor_copy(
                Bv(VTM + r * 1024, [(1, 1024)]),
                Pv(r * 1024, [(1, 1024)])), (vVt, 1))

    emit_wkv(0)
    for g in range(4):
        emit_kv_group("K", 0, g)
        emit_kv_evac("K", 0, g)
    for g in range(4):
        emit_kv_group("V", 0, g)
        emit_kv_evac("V", 0, g)
    emit_wkv(1)
    emit_vt(0)
    emit_vt_copies(0)

    for h in range(H):

        def emit_scores_tc(tc, h=h):
            pair = tc % 2
            if tc == 0:
                w("t", kv_evac[("K", h, 3)])
                w("t", vVt, 2 * h + 1)
            elif tc == 1:
                w("t", vVt, 2 * h + 2)
            else:
                w("t", sE, 16 * h + tc - 1)
            for half in range(2):
                ride = (tS, 1) if half == 1 else None
                op("t", lambda e, tc=tc, half=half, pair=pair, h=h: e.matmul(
                    Pv(pair * 1024 + half * 512, [(1, 512)]),
                    Bv(KSB + (h % 2) * 2048 + tc * 128, [(1, 128)]),
                    Bv(QF + h * 1024 + half * 512, [(1, 512)]),
                    start=True, stop=True), ride)
            r = tc // 4
            w("s", tS, 16 * h + tc + 1)
            w("s", tRd, 4 * h + r)
            w("s", vsm, 4 * h + r)
            op("s", lambda e, tc=tc, pair=pair: e.activation(
                Bv(PTR + (tc % 4) * 1024, [(1, 1024)]),
                Pv(pair * 1024, [(1, 1024)]),
                AF.Exp, bias=Fv(ST + 41, [(1, 1)]), scale=QK), (sE, 1))

        def emit_pv_round(r, h=h):
            w("t", sE, 16 * h + 4 * (r + 1))
            if r == 0:
                w("t", vO, h)
            for i in range(4):
                tcv = 4 * r + i
                for half in range(2):
                    ride = (tRd, 1) if (i == 3 and half == 1) else None
                    op("t", lambda e, tcv=tcv, i=i, half=half, r=r: e.matmul(
                        Pv(3072 + half * 512, [(1, 512)]),
                        Bv(VTM + tcv * 128, [(1, 128)]),
                        Bv(PTR + i * 1024 + half * 512, [(1, 512)]),
                        start=(r == 0 and i == 0), stop=(r == 3 and i == 3)), ride)

        def emit_tree_round(r, h=h):
            w("v", sE, 16 * h + 4 * (r + 1))
            with_lp = "bf16 softmax denominator tree (rvr impact <1e-5)"
            def t1f(e):
                with nc.allow_low_precision(with_lp):
                    return e.tensor_tensor(Bv(WDA, [(1, 1024)]),
                                           Bv(PTR, [(1, 1024)]),
                                           Bv(PTR + 1024, [(1, 1024)]), ALU.add)
            def t2f(e):
                with nc.allow_low_precision(with_lp):
                    return e.tensor_tensor(Bv(WDA + 1024, [(1, 1024)]),
                                           Bv(PTR + 2048, [(1, 1024)]),
                                           Bv(PTR + 3072, [(1, 1024)]), ALU.add)
            def t3f(e):
                with nc.allow_low_precision(with_lp):
                    return e.tensor_tensor(Bv(WDA, [(1, 1024)]),
                                           Bv(WDA, [(1, 1024)]),
                                           Bv(WDA + 1024, [(1, 1024)]), ALU.add)
            def accf(e, r=r):
                with nc.allow_low_precision(with_lp):
                    if r == 0:
                        return e.tensor_copy(Bv(RB, [(1, 1024)]),
                                             Bv(WDA, [(1, 1024)]))
                    return e.tensor_tensor(Bv(RB, [(1, 1024)]),
                                           Bv(RB, [(1, 1024)]),
                                           Bv(WDA, [(1, 1024)]), ALU.add)
            op("v", t1f, (vI, 1))
            op("v", t2f, (vI, 1))
            w("v", vI, vI.n)
            op("v", t3f, (vI, 1))
            w("v", vI, vI.n)
            op("v", accf, (vsm, 1))

        for r in range(4):
            emit_scores_tc(4 * r)
            emit_scores_tc(4 * r + 1)
            if r >= 1:
                emit_pv_round(r - 1)
            emit_scores_tc(4 * r + 2)
            emit_scores_tc(4 * r + 3)
            if h + 1 < H:
                if r < 2:
                    emit_kv_group("K", h + 1, 2 * r)
                    emit_kv_evac("K", h + 1, 2 * r)
                    emit_kv_group("K", h + 1, 2 * r + 1)
                    emit_kv_evac("K", h + 1, 2 * r + 1)
                else:
                    emit_kv_group("V", h + 1, 2 * (r - 2))
                    emit_kv_evac("V", h + 1, 2 * (r - 2))
                    emit_kv_group("V", h + 1, 2 * (r - 2) + 1)
                    emit_kv_evac("V", h + 1, 2 * (r - 2) + 1)
            emit_tree_round(r)
        emit_pv_round(3)
        if h + 2 < H:
            emit_wkv(h + 2)
        if h + 1 < H:
            emit_vt(h + 1)
            emit_vt_copies(h + 1)
        w("t", vsm, 4 * h + 4)
        if h + 1 < H:
            w("t", kv_evac[("V", h + 1, 3)])
        for half in range(2):
            ride = (tRB, 1) if half == 1 else None
            op("t", lambda e, half=half: e.matmul(
                Pv(2048 + half * 512, [(1, 512)]),
                Bv(ONESC, [(1, 128)]),
                Bv(RB + half * 512, [(1, 512)]),
                start=True, stop=True), ride)
        w("v", tRB, h + 1)
        if h >= 1:
            w("v", vO, h)
        op("v", lambda e: e.reciprocal(
            Fv(T1, [(1, 1024)]), Pv(2048, [(1, 1024)])), (vRc, 1))
        w("v", tRd, 4 * h + 4)
        w("v", vRc, vRc.n)
        op("v", lambda e, h=h: e.tensor_tensor(
            Bv(OFM + h * 1024, [(1, 1024)]),
            Pv(3072, [(1, 1024)]),
            Fv(T1, [(1, 1024)]), ALU.mult), (vO, 1))

    marks["C"] = {e: len(l) for e, l in ops.items()}
    w("g", tKV, 128)
    for gchunk in range(4):
        op("g", lambda e, gchunk=gchunk: e.dma_start(
            Bv(gchunk * 4 * 2048, [(2048, 4), (1, 2048)]),
            bass.AP(wo.tensor, gchunk * 4 * 128 * D,
                    [[D, 128], [128 * D, 4], [1, D]])), (dWO[gchunk], 16))
    marks["Dwo"] = {e: len(l) for e, l in ops.items()}

    for qc in range(NQC):
        slot = qc % 2
        if qc < 2:
            w("y", vXA, 16)
        else:
            w("y", vD, qc - 1)
        op("y", lambda e, qc=qc, slot=slot: e.dma_start(
            Fv(XS0 + slot * 2048, [(1, 2048)]),
            xb[qc * 128:(qc + 1) * 128, :]), (dXO[slot], 16))
        pair = qc % 2
        if qc == 0:
            w("t", sE, 256)
            w("t", vO, 16)
        elif qc == 1:
            w("t", vRc, 16)
        else:
            w("t", vD, qc - 1)
        for gchunk in range(4):
            w("t", dWO[gchunk], 16)
            for kc in range(4 * gchunk, 4 * gchunk + 4):
                for j in range(4):
                    ride = (tD, 1) if (kc == NKC - 1 and j == 3) else None
                    op("t", lambda e, kc=kc, j=j, qc=qc, pair=pair: e.matmul(
                        Pv(pair * 2048 + j * 512, [(1, 512)]),
                        Bv(OFM + kc * 1024 + qc * 128, [(1, 128)]),
                        Bv(kc * 2048 + j * 512, [(1, 512)]),
                        start=(kc == 0), stop=(kc == NKC - 1)), ride)
        if qc == 0:
            marks["Datt"] = {e: len(l) for e, l in ops.items()}
        ps_v = (lambda pair=pair: Pv(pair * 2048, [(1, 2048)]))
        par, M, RS = ln_site(
            src_v=ps_v, src_s=ps_v,
            dump=lambda: Bv(PTR, [(1, 2048)]),
            vwaits=[(tD, qc + 1)], swaits=[(tD, qc + 1)], psum_src=True)
        if qc == 0:
            marks["Dln2"] = {e: len(l) for e, l in ops.items()}
        w("v", vI, vI.n)
        if qc >= 1:
            w("v", dSP[(qc - 1) % 2], 16 * ((qc - 1) // 2 + 1))
        op("v", lambda e, pair=pair, par=par, M=M, RS=RS: e.tensor_scalar(
            Fv(T1, [(1, 2048)]),
            Pv(pair * 2048, [(1, 2048)]),
            stat(par, M), stat(par, RS), ALU.subtract, ALU.mult), (vI, 1))
        if qc == 0:
            marks["Dt1"] = {e: len(l) for e, l in ops.items()}
        w("v", dXO[slot], 16 * (qc // 2 + 1))
        w("v", vI, vI.n)
        op("v", lambda e, slot=slot: e.tensor_tensor(
            Fv(T1, [(1, 2048)]),
            Fv(T1, [(1, 2048)]),
            Fv(XS0 + slot * 2048, [(1, 2048)]), ALU.add), (vI, 1))
        if qc == 0:
            marks["Dx2"] = {e: len(l) for e, l in ops.items()}
        t1_v = (lambda: Fv(T1, [(1, 2048)]))
        w("s", vI, vI.n)
        par3, M3, RS3 = ln_site(
            src_v=t1_v, src_s=t1_v,
            dump=lambda: Bv(PTR, [(1, 2048)]),
            vwaits=[(vI, vI.n)], swaits=[(vI, vI.n)])
        if qc == 0:
            marks["Dln3"] = {e: len(l) for e, l in ops.items()}
        w("v", vI, vI.n)
        op("v", lambda e, qc=qc, par3=par3, M3=M3, RS3=RS3: e.tensor_scalar(
            Bv(QF + qc * 2048, [(1, 2048)]),
            Fv(T1, [(1, 2048)]),
            stat(par3, M3), stat(par3, RS3), ALU.subtract, ALU.mult), (vD, 1))
        w("g", vD, qc + 1)
        op("g", lambda e, qc=qc: e.dma_start(
            x2sa[qc * 128:(qc + 1) * 128, :],
            Fv(T1, [(1, 2048)])), (dSP[qc % 2], 16))
        marks["D%d" % qc] = {e: len(l) for e, l in ops.items()}

    marks["D"] = {e: len(l) for e, l in ops.items()}
    for qc in range(NQC):
        if qc == 0:
            w("t", vD, 7)
        elif qc == 1:
            w("t", vD, 8)
        else:
            w("t", vA3, qc - 1)
        w("t", vD, min(qc + 1, 8))
        pair = qc % 2
        for kc in range(NKC):
            ride = (tT3, 1) if kc == NKC - 1 else None
            op("t", lambda e, qc=qc, kc=kc, pair=pair: e.matmul(
                Pv(pair * 2048 + kc * 128, [(1, 128)]),
                Bv(QF + qc * 2048 + kc * 128, [(1, 128)]),
                Bv(IDENT, [(1, 128)]),
                start=True, stop=True), ride)
        w("v", tT3, qc + 1)
        op("v", lambda e, qc=qc, pair=pair: e.tensor_copy(
            Bv(OFM + qc * 128, [(1024, 16), (1, 128)]),
            Pv(pair * 2048, [(128, 16), (1, 128)])), (vA3, 1))

    for hc in range(NHC):
        m_g = 48 + 2 * hc
        slot_g, dwv_g = wdma(wg[hc, :, :], m_g, consumed_at.get(m_g - 3))
        m_u = 49 + 2 * hc
        slot_u, dwv_u = wdma(wu[hc, :, :], m_u, consumed_at.get(m_u - 3))

        gb = 4 * (hc % 2)
        ub = 4 * (hc % 2) + 2
        w("t", dW[slot_g], dwv_g)
        if hc == 0:
            w("t", vA3, 8)
        if hc >= 2:
            w("t", sSil, hc - 1)
        for kc in range(NKC):
            for half in range(2):
                ride = (tE, 1) if (kc == NKC - 1 and half == 1) else None
                op("t", lambda e, slot=slot_g, kc=kc, half=half, gb=gb: e.matmul(
                    Pv(gb * 512 + half * 512, [(1, 512)]),
                    Bv(W3 + slot * 2048 + kc * 128, [(1, 128)]),
                    Bv(OFM + kc * 1024 + half * 512, [(1, 512)]),
                    start=(kc == 0), stop=(kc == NKC - 1)), ride)
        consumed_at[m_g] = (tE, tE.n)
        w("t", dW[slot_u], dwv_u)
        if hc >= 2:
            w("t", vGa, hc - 1)
        for kc in range(NKC):
            for half in range(2):
                ride = (tE, 1) if (kc == NKC - 1 and half == 1) else None
                op("t", lambda e, slot=slot_u, kc=kc, half=half, ub=ub: e.matmul(
                    Pv(ub * 512 + half * 512, [(1, 512)]),
                    Bv(W3 + slot * 2048 + kc * 128, [(1, 128)]),
                    Bv(OFM + kc * 1024 + half * 512, [(1, 512)]),
                    start=(kc == 0), stop=(kc == NKC - 1)), ride)
        consumed_at[m_u] = (tE, tE.n)
        w("s", tE, 2 * hc + 1)
        if hc == 0:
            w("s", dSP[0], 16 * 4)
            w("s", dSP[1], 16 * 4)
        w("s", vGa, hc)
        op("s", lambda e, gb=gb: e.activation(
            Fv(T1, [(1, 1024)]),
            Pv(gb * 512, [(1, 1024)]), AF.Silu), (sSil, 1))
        w("v", tE, 2 * hc + 2)
        w("v", sSil, hc + 1)
        op("v", lambda e, hc=hc, ub=ub: e.tensor_tensor(
            Bv(GMAP[hc], [(1, 1024)]),
            Fv(T1, [(1, 1024)]),
            Pv(ub * 512, [(1, 1024)]), ALU.mult), (vGa, 1))

    marks["E"] = {e: len(l) for e, l in ops.items()}
    for j in range(4):
        for qd in range(16):
            Q = 16 * j + qd
            base = WDA if Q % 2 == 0 else WDB
            if Q >= 2:
                w("g", tF, Q - 1)
            op("g", lambda e, j=j, qd=qd, base=base: e.dma_start(
                Bv(base, [(512, 4), (1, 512)]),
                bass.AP(wd.tensor, (j * NHC + 4 * qd) * 128 * 512,
                        [[512, 128], [128 * 512, 4], [1, 512]])),
               (dWD[Q % 2], 16))
        for qd in range(16):
            Q = 16 * j + qd
            base = WDA if Q % 2 == 0 else WDB
            w("t", dWD[Q % 2], 16 * (Q // 2 + 1))
            if Q == 0:
                w("t", sSil, 64)
                w("t", vGa, 64)
            if qd == 0 and j >= 1:
                w("t", sFe, 4 * j)
                w("t", vFe, 4 * j)
            for i in range(4):
                hc = 4 * qd + i
                for qc in range(NQC):
                    ride = (tF, 1) if (i == 3 and qc == NQC - 1) else None
                    op("t", lambda e, hc=hc, qc=qc, i=i, base=base, j=j: e.matmul(
                        Pv(qc * 512, [(1, 512)]),
                        Bv(GMAP[hc] + qc * 128, [(1, 128)]),
                        Bv(base + i * 512, [(1, 512)]),
                        start=(hc == 0), stop=(hc == NHC - 1)), ride)
        w("s", tF, 16 * (j + 1))
        for qc in range(4):
            op("s", lambda e, qc=qc, j=j: e.activation(
                Bv(OFM + qc * 2048 + j * 512, [(1, 512)]),
                Pv(qc * 512, [(1, 512)]), AF.Copy), (sFe, 1))
        w("s", sFe, sFe.n)
        for qc in range(4):
            w("s", sFs, sFs.n)
            op("s", lambda e, qc=qc, j=j: e.activation(
                Fv(XS0, [(1, 512)]),
                Bv(OFM + qc * 2048 + j * 512, [(1, 512)]),
                AF.Square, accum_out=Fv(ST + 96 + qc * 4 + j, [(1, 1)])), (sFs, 1))
            w("s", sFs, sFs.n)
            op("s", lambda e, qc=qc, j=j: e.activation(
                Fv(XS0, [(1, 512)]),
                Bv(OFM + qc * 2048 + j * 512, [(1, 512)]),
                AF.Identity, accum_out=Fv(ST + 64 + qc * 4 + j, [(1, 1)])), (sFs, 1))
        w("v", tF, 16 * (j + 1))
        for qc in range(4, 8):
            op("v", lambda e, qc=qc, j=j: e.tensor_copy(
                Bv(OFM + qc * 2048 + j * 512, [(1, 512)]),
                Pv(qc * 512, [(1, 512)])), (vFe, 1))
        w("v", vFe, vFe.n)
        for qc in range(4, 8):
            op("v", lambda e, qc=qc, j=j: e.reduce_sum(
                Fv(ST + 64 + qc * 4 + j, [(1, 1)]),
                Bv(OFM + qc * 2048 + j * 512, [(1, 512)]), axis=AX.X), (vFs, 1))
            w("v", vFs, vFs.n)
            def sq(e, qc=qc, j=j):
                return e.scalar_tensor_tensor(
                    Fv(XS1, [(1, 512)]),
                    Bv(OFM + qc * 2048 + j * 512, [(1, 512)]),
                    1.0,
                    Bv(OFM + qc * 2048 + j * 512, [(1, 512)]),
                    ALU.mult, ALU.mult,
                    accum_out=Fv(ST + 96 + qc * 4 + j, [(1, 1)]))
            op("v", sq, (vFs, 1))

    marks["F"] = {e: len(l) for e, l in ops.items()}
    w("g", dSP[0], 16 * 4)
    w("g", dSP[1], 16 * 4)
    op("g", lambda e: e.dma_start(out[:, :], x2sa[:, :]), (dOC, 16))

    for qc in range(NQC):
        par = qc % 4
        SM, SQ2, M, M2, VE, SD, RS, MR = 0, 1, 2, 3, 4, 5, 6, 7
        if qc == 0:
            w("v", sFs, 32)
            w("v", vFs, 32)
            w("v", sFe, 16)
            w("v", vFe, 16)
            w("v", tF, 64)
            w("s", sFs, 32)
            w("s", vFs, 32)
            w("s", sFe, 16)
            w("s", vFe, 16)
            w("s", tF, 64)
        w("v", vI, vI.n)
        op("v", lambda e, par=par, qc=qc: e.tensor_tensor(
            stat(par, SM), Fv(ST + 64 + qc * 4, [(1, 1)]),
            Fv(ST + 64 + qc * 4 + 1, [(1, 1)]), ALU.add), (vI, 1))
        op("v", lambda e, par=par, qc=qc: e.tensor_tensor(
            stat(par, M2), Fv(ST + 64 + qc * 4 + 2, [(1, 1)]),
            Fv(ST + 64 + qc * 4 + 3, [(1, 1)]), ALU.add), (vI, 1))
        w("v", vI, vI.n)
        op("v", lambda e, par=par: e.tensor_tensor(
            stat(par, SM), stat(par, SM), stat(par, M2), ALU.add), (vI, 1))
        op("v", lambda e, par=par, qc=qc: e.tensor_tensor(
            stat(par, SQ2), Fv(ST + 96 + qc * 4, [(1, 1)]),
            Fv(ST + 96 + qc * 4 + 1, [(1, 1)]), ALU.add), (vI, 1))
        w("v", vI, vI.n)
        op("v", lambda e, par=par, qc=qc: e.tensor_tensor(
            stat(par, M2), Fv(ST + 96 + qc * 4 + 2, [(1, 1)]),
            Fv(ST + 96 + qc * 4 + 3, [(1, 1)]), ALU.add), (vI, 1))
        w("v", vI, vI.n)
        op("v", lambda e, par=par: e.tensor_tensor(
            stat(par, SQ2), stat(par, SQ2), stat(par, M2), ALU.add), (vI, 1))
        w("v", vI, vI.n)
        op("v", lambda e, par=par: e.tensor_scalar(
            stat(par, M), stat(par, SM), 1.0 / D, None, ALU.mult), (vI, 1))
        w("v", vI, vI.n)
        op("v", lambda e, par=par: e.tensor_scalar(
            stat(par, M2), stat(par, M), stat(par, M), None, ALU.mult), (vI, 1))
        w("v", vI, vI.n)
        op("v", lambda e, par=par: e.scalar_tensor_tensor(
            stat(par, VE), stat(par, SQ2), 1.0 / D, stat(par, M2),
            ALU.mult, ALU.subtract), (vI, 1))
        w("s", vI, vI.n)
        op("s", lambda e, par=par: e.activation(
            stat(par, SD), stat(par, VE), AF.Sqrt,
            bias=Fv(ST + 40, [(1, 1)])), (sSd, 1))
        w("v", sSd, sSd.n)
        op("v", lambda e, par=par: e.reciprocal(stat(par, RS), stat(par, SD)), (vI, 1))
        w("v", vI, vI.n)
        op("v", lambda e, par=par: e.tensor_scalar(
            stat(par, MR), stat(par, M),
            stat(par, RS), -1.0, ALU.mult, ALU.mult), (vI, 1))
        slot = qc % 3
        w("s", vI, vI.n)
        if qc >= 3:
            w("s", dAC[slot], 16 * (qc // 3))
        op("s", lambda e, qc=qc, slot=slot, par=par: e.activation(
            Fv(slot * 2048, [(1, 2048)]),
            Bv(OFM + qc * 2048, [(1, 2048)]),
            AF.Identity, bias=stat(par, MR), scale=stat(par, RS)), (vG, 1))
        w("g", dOC, 16)
        w("g", vG, qc + 1)
        op("g", lambda e, qc=qc, slot=slot: e.dma_start(
            out[qc * 128:(qc + 1) * 128, :],
            Fv(slot * 2048, [(1, 2048)]),
            accum_op=ALU.add), (dAC[slot], 16))

    w("g", dAC[0], 16 * 3)
    w("g", dAC[1], 16 * 3)
    w("g", dAC[2], 16 * 2)
    w("g", dOC, 16)

    marks["G"] = {e: len(l) for e, l in ops.items()}
    assert tKV.n == 128 and tS.n == 256 and tRd.n == 64
    assert tQ.n == 16 and sQe.n == 16 and sKV.n == 128 and vO.n == 16
    assert tD.n == 8 and vD.n == 8 and tE.n == 128 and vGa.n == 64
    assert tF.n == 64 and sFe.n == 16 and vFe.n == 16 and vG.n == 8
    assert sFs.n == 32 and vFs.n == 32
    assert tTR.n == 16 and vA.n == 16 and vXA.n == 16 and tT3.n == 8 and vA3.n == 8
    assert tVt.n == 32 and vVt.n == 32 and tRB.n == 16 and vRc.n == 16 and sE.n == 256
    assert vsm.n == 64

    cut = marks[_STOP_AFTER]
    with nc.Block() as block:
        @block.tensor
        def _(eng):
            for f in ops["t"][:cut["t"]]:
                f(eng)

        @block.vector
        def _(eng):
            for f in ops["v"][:cut["v"]]:
                f(eng)

        @block.scalar
        def _(eng):
            for f in ops["s"][:cut["s"]]:
                f(eng)

        @block.gpsimd
        def _(eng):
            for f in ops["g"][:cut["g"]]:
                f(eng)

        @block.sync
        def _(eng):
            for f in ops["y"][:cut["y"]]:
                f(eng)

    es.close()
    return nc


_NC_CACHE = [None]


def _host_layouts(w_q, w_k, w_v, w_o, w_gate, w_up, w_down):
    def qkv(wm):
        return np.ascontiguousarray(
            wm.reshape(H, 128, NKC, 128).transpose(0, 3, 2, 1)).reshape(H, 128, D)

    wq_t = qkv(w_q)
    wk_t = qkv(w_k)
    wv_t = qkv(w_v)
    wo_t = np.ascontiguousarray(
        w_o.reshape(D, NKC, 128).transpose(1, 2, 0))
    def gu(wm):
        return np.ascontiguousarray(
            wm.reshape(NHC, 128, NKC, 128).transpose(0, 3, 2, 1)).reshape(NHC, 128, D)
    wg_t = gu(w_gate)
    wu_t = gu(w_up)
    wd_t = np.ascontiguousarray(
        w_down.reshape(4, 512, NHC, 128).transpose(0, 2, 3, 1))
    return wq_t, wk_t, wv_t, wo_t, wg_t, wu_t, wd_t


def kernel(L0_attn_x, L0_attn_w_q, L0_attn_w_k, L0_attn_w_v, L0_attn_w_o,
           L0_ffn_w_gate, L0_ffn_w_up, L0_ffn_w_down):
    x = np.asarray(L0_attn_x, dtype=np.float32)
    wq_t, wk_t, wv_t, wo_t, wg_t, wu_t, wd_t = _host_layouts(
        np.asarray(L0_attn_w_q, np.float32), np.asarray(L0_attn_w_k, np.float32),
        np.asarray(L0_attn_w_v, np.float32), np.asarray(L0_attn_w_o, np.float32),
        np.asarray(L0_ffn_w_gate, np.float32), np.asarray(L0_ffn_w_up, np.float32),
        np.asarray(L0_ffn_w_down, np.float32))
    idn = np.eye(128, dtype=np.float32)
    onc = np.ones((128, 128), np.float32)
    onr = np.ones((1, 128), np.float32)
    cstm = np.tile(np.array([[EPS, -SH]], np.float32), (128, 1))

    if _NC_CACHE[0] is None:
        _NC_CACHE[0] = build_nc()
    nc = _NC_CACHE[0]

    shared = {"wq": wq_t, "wk": wk_t, "wv": wv_t, "wo": wo_t,
              "wg": wg_t, "wu": wu_t, "wd": wd_t,
              "idn": idn, "onc": onc, "onr": onr, "cst": cstm}
    in_maps = []
    for core in range(8):
        b, half = core // 2, core % 2
        own = x[b, half * NQ:(half + 1) * NQ, :]
        oth = x[b, (1 - half) * NQ:(2 - half) * NQ, :]
        xbp = np.ascontiguousarray(np.concatenate([own, oth], axis=0))
        m = dict(shared)
        m["xb"] = xbp
        in_maps.append(m)

    res = run_bass_kernel_spmd(nc, in_maps, core_ids=list(range(8)))
    full = np.empty((4, S, D), np.float32)
    for core in range(8):
        b, half = core // 2, core % 2
        full[b, half * NQ:(half + 1) * NQ, :] = res.results[core]["out"]
    return full


# version 13
# speedup vs baseline: 1.0087x; 1.0087x over previous
import numpy as np
import concourse.bass as bass
import concourse.mybir as mybir
from concourse.bass_utils import run_bass_kernel_spmd

F32 = mybir.dt.float32
BF16 = mybir.dt.bfloat16
AF = mybir.ActivationFunctionType
ALU = mybir.AluOpType
AX = mybir.AxisListType

D = 2048
NKC = 16
H = 16
S = 2048
NQ = 1024
NQC = 8
NTC = 16
NHC = 64
EPS = 1e-5
QK = float(np.float32(1.0) / np.float32(np.sqrt(128.0)))
SH = 3.0

LN1 = 0
QF = 32768
PTR = 49152
KSB = 53248
VFM = 57344
VTM = 59392
W3 = 61440
WDA = 67584
IDENT = 69632
ONESC = 69760
ONESR = 69888
RB = 70016
RECIP = 71040
OFM = 72064
TAIL = 88448
ABF_N = 92544

XS0 = 0
XS1 = 2048
T1 = 4096
ST = 6144
AF32_N = 6656

GMAP = {}
for _hc in range(32):
    GMAP[_hc] = 1024 * _hc
for _hc in range(32, 48):
    GMAP[_hc] = QF + 1024 * (_hc - 32)
GMAP[48] = 51200
GMAP[49] = 52224
for _hc in range(50, 54):
    GMAP[_hc] = KSB + 1024 * (_hc - 50)
GMAP[54] = VFM
GMAP[55] = VFM + 1024
GMAP[56] = VTM
GMAP[57] = VTM + 1024
GMAP[58] = RB
GMAP[59] = RECIP
for _hc in range(60, 64):
    GMAP[_hc] = TAIL + 1024 * (_hc - 60)
WDB = PTR


_STOP_AFTER = "G"


class Ctr:
    __slots__ = ("s", "n")

    def __init__(self, s):
        self.s = s
        self.n = 0


def build_nc():
    nc = bass.Bass("TRN2", target_bir_lowering=False)
    xb = nc.dram_tensor("xb", [S, D], F32, kind="ExternalInput").ap()
    wq = nc.dram_tensor("wq", [H, 128, D], F32, kind="ExternalInput").ap()
    wk = nc.dram_tensor("wk", [H, 128, D], F32, kind="ExternalInput").ap()
    wv = nc.dram_tensor("wv", [H, 128, D], F32, kind="ExternalInput").ap()
    wo = nc.dram_tensor("wo", [NKC, 128, D], F32, kind="ExternalInput").ap()
    wg = nc.dram_tensor("wg", [NHC, 128, D], F32, kind="ExternalInput").ap()
    wu = nc.dram_tensor("wu", [NHC, 128, D], F32, kind="ExternalInput").ap()
    wd = nc.dram_tensor("wd", [4, NHC, 128, 512], F32, kind="ExternalInput").ap()
    idn = nc.dram_tensor("idn", [128, 128], F32, kind="ExternalInput").ap()
    onc = nc.dram_tensor("onc", [128, 128], F32, kind="ExternalInput").ap()
    onr = nc.dram_tensor("onr", [1, 128], F32, kind="ExternalInput").ap()
    cst = nc.dram_tensor("cst", [128, 2], F32, kind="ExternalInput").ap()
    x2s = nc.dram_tensor("x2s", [NQ, D], F32)
    out = nc.dram_tensor("out", [NQ, D], F32, kind="ExternalOutput").ap()
    x2sa = x2s.ap()

    import contextlib
    es = contextlib.ExitStack()
    ABF = es.enter_context(nc.sbuf_tensor("abf", [128, ABF_N], BF16))
    AFP = es.enter_context(nc.sbuf_tensor("afp", [128, AF32_N], F32))
    PS = es.enter_context(nc.psum_tensor("ps", [128, 4096], F32))

    def Bv(off, dims, parts=128):
        return bass.AP(ABF, off, [[ABF_N, parts]] + [list(d) for d in dims])

    def Fv(off, dims, parts=128):
        return bass.AP(AFP, off, [[AF32_N, parts]] + [list(d) for d in dims])

    def Pv(off, dims, parts=128):
        return bass.AP(PS, off, [[4096, parts]] + [list(d) for d in dims])

    def stat(par, k):
        return Fv(ST + par * 8 + k, [(1, 1)])

    sems = {}

    def C(name):
        c = Ctr(es.enter_context(nc.semaphore(name)))
        sems[name] = c
        return c

    dM = C("dM")
    dW = [C("dW0"), C("dW1"), C("dW2")]
    dWO = [C("dWO0"), C("dWO1"), C("dWO2"), C("dWO3")]
    dWD = [C("dWDa"), C("dWDb")]
    dOC = C("dOC")
    dAC = [C("dAC0"), C("dAC1"), C("dAC2")]
    dXA = [C("dXA0"), C("dXA1"), C("dXA2")]
    dXO = [C("dXO0"), C("dXO1")]
    dSP = [C("dSP0"), C("dSP1")]
    tTR = C("tTR")
    tQ = C("tQ")
    tKV = C("tKV")
    tVt = C("tVt")
    tS = C("tS")
    tRd = C("tRd")
    tRB = C("tRB")
    tD = C("tD")
    tT3 = C("tT3")
    tE = C("tE")
    tF = C("tF")
    sI = C("sI")
    sSq = C("sSq")
    sSd = C("sSd")
    sQe = C("sQe")
    sKV = C("sKV")
    sE = C("sE")
    sRB = C("sRB")
    sSm = C("sSm")
    sSil = C("sSil")
    sFe = C("sFe")
    vI = C("vI")
    vXA = C("vXA")
    vA = C("vA")
    vVt = C("vVt")
    vRc = C("vRc")
    vO = C("vO")
    vD = C("vD")
    vA3 = C("vA3")
    vGa = C("vGa")
    vFe = C("vFe")
    vG = C("vG")
    sFs = C("sFs")
    vFs = C("vFs")

    ops = {"t": [], "v": [], "s": [], "g": [], "y": []}
    marks = {}

    def op(e, f, ride=None):
        pair = None
        if ride is not None:
            c, k = ride
            c.n += k
            pair = (c.s, k)
        if pair is None:
            ops[e].append(f)
        else:
            def g(eng, f=f, pair=pair):
                f(eng).then_inc(*pair)
            ops[e].append(g)

    def w(e, c, v=None):
        if v is None:
            c, v = c
        ops[e].append(lambda eng, s=c.s, v=v: eng.wait_ge(s, v))

    site_par = [0]

    def ln_site(src_v, src_s, dump, vwaits, swaits, psum_src=False):
        par = site_par[0] % 4
        site_par[0] += 1
        SUM, M, M2, VE, SD, RS, SQ = range(7)
        for c, v in vwaits:
            w("v", c, v)
        op("v", lambda e: e.reduce_sum(stat(par, SUM), src_v(), axis=AX.X), (vI, 1))
        for c, v in swaits:
            w("s", c, v)
        if psum_src:
            w("s", vI, vI.n)
        op("s", lambda e: e.activation(dump(), src_s(), AF.Square,
                                       accum_out=stat(par, SQ)), (sSq, 1))
        w("v", vI, vI.n)
        op("v", lambda e: e.tensor_scalar(stat(par, M), stat(par, SUM),
                                          1.0 / D, None, ALU.mult), (vI, 1))
        w("v", vI, vI.n)
        op("v", lambda e: e.tensor_scalar(stat(par, M2), stat(par, M),
                                          stat(par, M), None, ALU.mult), (vI, 1))
        w("v", sSq, sSq.n)
        w("v", vI, vI.n)
        op("v", lambda e: e.scalar_tensor_tensor(stat(par, VE), stat(par, SQ),
                                                 1.0 / D, stat(par, M2),
                                                 ALU.mult, ALU.subtract), (vI, 1))
        w("s", vI, vI.n)
        op("s", lambda e: e.activation(stat(par, SD), stat(par, VE), AF.Sqrt,
                                       bias=Fv(ST + 40, [(1, 1)])), (sSd, 1))
        w("v", sSd, sSd.n)
        op("v", lambda e: e.reciprocal(stat(par, RS), stat(par, SD)), (vI, 1))
        return par, M, RS

    op("g", lambda e: e.dma_start(Bv(IDENT, [(1, 128)]), idn[:, :]), (dM, 16))
    op("g", lambda e: e.dma_start(Bv(ONESC, [(1, 128)]), onc[:, :]), (dM, 16))
    op("g", lambda e: e.dma_start(Bv(ONESR, [(1, 128)], parts=1), onr[:, :]), (dM, 16))
    op("g", lambda e: e.dma_start(Fv(ST + 40, [(1, 2)]), cst[:, :],
                                  allow_slow_non_contiguous=True), (dM, 16))
    w("s", dM, 64)

    def a_stage1(tc):
        slot = tc % 3
        if tc >= 3:
            w("y", vXA, tc - 2)
        op("y", lambda e, slot=slot, tc=tc: e.dma_start(
            Fv(XS0 + slot * 2048, [(1, 2048)]),
            xb[tc * 128:(tc + 1) * 128, :]), (dXA[slot], 16))
        xs_v = (lambda slot=slot: Fv(XS0 + slot * 2048, [(1, 2048)]))
        par, M, RS = ln_site(
            src_v=xs_v, src_s=xs_v,
            dump=lambda: Bv(KSB, [(1, 2048)]),
            vwaits=[(dXA[slot], 16 * (tc // 3 + 1))],
            swaits=[(dXA[slot], 16 * (tc // 3 + 1))])
        w("v", vI, vI.n)
        op("v", lambda e, par=par, M=M, RS=RS: e.tensor_scalar(
            stat(par, 7), stat(par, M),
            stat(par, RS), -1.0, ALU.mult, ALU.mult), (vI, 1))
        w("s", vI, vI.n)
        if tc >= 2:
            w("s", tTR, tc - 1)
        op("s", lambda e, slot=slot, par=par, RS=RS: e.activation(
            Bv(OFM + (tc % 2) * 2048, [(1, 2048)]),
            Fv(XS0 + slot * 2048, [(1, 2048)]),
            AF.Identity, bias=stat(par, 7), scale=stat(par, RS)), (vXA, 1))

    def a_stage2(tc):
        slot = tc % 2
        if tc == 0:
            w("t", dM, 64)
        w("t", vXA, tc + 1)
        if tc >= 1:
            w("t", vA, tc)
        for kc in range(NKC):
            ride = (tTR, 1) if kc == NKC - 1 else None
            op("t", lambda e, slot=slot, kc=kc: e.matmul(
                Pv(kc * 128, [(1, 128)]),
                Bv(OFM + slot * 2048 + kc * 128, [(1, 128)]),
                Bv(IDENT, [(1, 128)]),
                start=True, stop=True), ride)
        w("v", tTR, tc + 1)
        op("v", lambda e, tc=tc: e.tensor_copy(
            Bv(LN1 + tc * 128, [(2048, 16), (1, 128)]),
            Pv(0, [(128, 16), (1, 128)])), (vA, 1))

    a_stage1(0)
    a_stage1(1)
    for tc in range(2, 8):
        a_stage1(tc)
        a_stage2(tc - 2)
    a_stage2(6)
    a_stage2(7)

    marks["A"] = {e: len(l) for e, l in ops.items()}
    consumed_at = {}
    wslot_cnt = [0, 0, 0]

    def wdma(src_ap, item, cons_wait):
        slot = item % 3
        if cons_wait is not None:
            w("g", cons_wait[0], cons_wait[1])
        wslot_cnt[slot] += 1
        op("g", lambda e, slot=slot, src_ap=src_ap: e.dma_start(
            Bv(W3 + slot * 2048, [(1, 2048)]), src_ap), (dW[slot], 16))
        return slot, 16 * wslot_cnt[slot]

    for h in range(H):
        if h < 8:
            a_stage1(8 + h)
        if 2 <= h < 10:
            a_stage2(6 + h)
        cons = (tQ, h - 2) if h >= 3 else None
        slot, dwv = wdma(wq[h, :, :], h, cons)
        if h == 0:
            w("t", vA, 8)
        w("t", dW[slot], dwv)
        if h >= 2:
            w("t", sQe, h - 1)
        pair = h % 2
        for kc in range(NKC):
            for half in range(2):
                ride = (tQ, 1) if (kc == NKC - 1 and half == 1) else None
                op("t", lambda e, slot=slot, kc=kc, half=half, pair=pair: e.matmul(
                    Pv(2048 + pair * 1024 + half * 512, [(1, 512)]),
                    Bv(W3 + slot * 2048 + kc * 128, [(1, 128)]),
                    Bv(LN1 + kc * 2048 + half * 512, [(1, 512)]),
                    start=(kc == 0), stop=(kc == NKC - 1)), ride)
        consumed_at[h] = (tQ, tQ.n)
        w("s", tQ, h + 1)
        op("s", lambda e, h=h, pair=pair: e.activation(
            Bv(QF + h * 1024, [(1, 1024)]),
            Pv(2048 + pair * 1024, [(1, 1024)]), AF.Copy), (sQe, 1))

    marks["B"] = {e: len(l) for e, l in ops.items()}
    kv_evac = {}
    kv_mm = {}
    vt_mm = {}
    wk_slot = {}
    wv_slot = {}
    vsm = C("vSm")

    def emit_wkv(h):
        m_k = 16 + 2 * h
        cons_k = consumed_at.get(m_k - 3)
        wk_slot[h] = wdma(wk[h, :, :], m_k, cons_k)
        m_v = 17 + 2 * h
        cons_v = consumed_at.get(m_v - 3)
        wv_slot[h] = wdma(wv[h, :, :], m_v, cons_v)

    def emit_kv_group(kind, h, g):
        slot, dwv = wk_slot[h] if kind == "K" else wv_slot[h]
        if g == 0:
            w("t", dW[slot], dwv)
        if kind == "K":
            if h == 0:
                w("t", vA, 4 * (g + 1))
            if h == 0 and g < 2:
                w("t", sQe, 16)
            elif g < 2 and h >= 1:
                w("t", kv_evac[("V", h - 1, g + 2)])
            elif g >= 2:
                w("t", kv_evac[("K", h, g - 2)])
            if h >= 2 and g < 2:
                w("t", vRc, h - 1)
        else:
            w("t", kv_evac[("K", h, g + 2)] if g < 2 else kv_evac[("V", h, g - 2)])
        bank = 4 + (g % 2)
        for kc in range(NKC):
            ride = (tKV, 1) if kc == NKC - 1 else None
            op("t", lambda e, slot=slot, kc=kc, g=g, bank=bank: e.matmul(
                Pv(bank * 512, [(1, 512)]),
                Bv(W3 + slot * 2048 + kc * 128, [(1, 128)]),
                Bv(LN1 + kc * 2048 + g * 512, [(1, 512)]),
                start=(kc == 0), stop=(kc == NKC - 1)), ride)
        kv_mm[(kind, h, g)] = (tKV, tKV.n)
        if kind == "K" and g == 3:
            consumed_at[16 + 2 * h] = (tKV, tKV.n)
        if kind == "V" and g == 3:
            consumed_at[17 + 2 * h] = (tKV, tKV.n)

    def emit_kv_evac(kind, h, g):
        w("s", kv_mm[(kind, h, g)])
        if kind == "V" and g == 0 and h >= 1:
            w("s", vt_mm[(h - 1, 1)])
        bank = 4 + (g % 2)
        dst = (KSB + (h % 2) * 2048) if kind == "K" else VFM
        op("s", lambda e, g=g, dst=dst, bank=bank: e.activation(
            Bv(dst + g * 512, [(1, 512)]),
            Pv(bank * 512, [(1, 512)]), AF.Copy), (sKV, 1))
        kv_evac[(kind, h, g)] = (sKV, sKV.n)

    def emit_vt(h):
        for r in range(2):
            w("t", kv_evac[("V", h, 2 * r + 1)])
            if r == 0 and h == 0:
                w("t", vA, 16)
            if r == 0 and h >= 1:
                w("t", sE, 16 * h)
            for i in range(8):
                tcv = r * 8 + i
                ride = (tVt, 1) if i == 7 else None
                op("t", lambda e, tcv=tcv, r=r, i=i: e.matmul(
                    Pv(r * 1024 + i * 128, [(1, 128)]),
                    Bv(VFM + tcv * 128, [(1, 128)]),
                    Bv(IDENT, [(1, 128)]),
                    start=True, stop=True), ride)
            vt_mm[(h, r)] = (tVt, tVt.n)

    def emit_vt_copies(h):
        for r in range(2):
            w("v", vt_mm[(h, r)])
            if h >= 1 and r == 0:
                w("v", tRd, 4 * h)
            op("v", lambda e, r=r: e.tensor_copy(
                Bv(VTM + r * 1024, [(1, 1024)]),
                Pv(r * 1024, [(1, 1024)])), (vVt, 1))

    emit_wkv(0)
    for g in range(4):
        emit_kv_group("K", 0, g)
        emit_kv_evac("K", 0, g)
    for g in range(4):
        emit_kv_group("V", 0, g)
        emit_kv_evac("V", 0, g)
    emit_wkv(1)
    emit_vt(0)
    emit_vt_copies(0)

    for h in range(H):

        def emit_scores_tc(tc, h=h):
            pair = tc % 2
            if tc == 0:
                w("t", kv_evac[("K", h, 3)])
                w("t", vVt, 2 * h + 1)
            elif tc == 1:
                w("t", vVt, 2 * h + 2)
            else:
                w("t", sE, 16 * h + tc - 1)
            for half in range(2):
                ride = (tS, 1) if half == 1 else None
                op("t", lambda e, tc=tc, half=half, pair=pair, h=h: e.matmul(
                    Pv(pair * 1024 + half * 512, [(1, 512)]),
                    Bv(KSB + (h % 2) * 2048 + tc * 128, [(1, 128)]),
                    Bv(QF + h * 1024 + half * 512, [(1, 512)]),
                    start=True, stop=True), ride)
            r = tc // 4
            w("s", tS, 16 * h + tc + 1)
            w("s", tRd, 4 * h + r)
            w("s", vsm, 4 * h + r)
            op("s", lambda e, tc=tc, pair=pair: e.activation(
                Bv(PTR + (tc % 4) * 1024, [(1, 1024)]),
                Pv(pair * 1024, [(1, 1024)]),
                AF.Exp, bias=Fv(ST + 41, [(1, 1)]), scale=QK), (sE, 1))

        def emit_pv_round(r, h=h):
            w("t", sE, 16 * h + 4 * (r + 1))
            if r == 0:
                w("t", vO, h)
            for i in range(4):
                tcv = 4 * r + i
                for half in range(2):
                    ride = (tRd, 1) if (i == 3 and half == 1) else None
                    op("t", lambda e, tcv=tcv, i=i, half=half, r=r: e.matmul(
                        Pv(3072 + half * 512, [(1, 512)]),
                        Bv(VTM + tcv * 128, [(1, 128)]),
                        Bv(PTR + i * 1024 + half * 512, [(1, 512)]),
                        start=(r == 0 and i == 0), stop=(r == 3 and i == 3)), ride)

        def emit_tree_round(r, h=h):
            w("v", sE, 16 * h + 4 * (r + 1))
            with_lp = "bf16 softmax denominator tree (rvr impact <1e-5)"
            def t1f(e):
                with nc.allow_low_precision(with_lp):
                    return e.tensor_tensor(Bv(WDA, [(1, 1024)]),
                                           Bv(PTR, [(1, 1024)]),
                                           Bv(PTR + 1024, [(1, 1024)]), ALU.add)
            def t2f(e):
                with nc.allow_low_precision(with_lp):
                    return e.tensor_tensor(Bv(WDA + 1024, [(1, 1024)]),
                                           Bv(PTR + 2048, [(1, 1024)]),
                                           Bv(PTR + 3072, [(1, 1024)]), ALU.add)
            def t3f(e):
                with nc.allow_low_precision(with_lp):
                    return e.tensor_tensor(Bv(WDA, [(1, 1024)]),
                                           Bv(WDA, [(1, 1024)]),
                                           Bv(WDA + 1024, [(1, 1024)]), ALU.add)
            def accf(e, r=r):
                with nc.allow_low_precision(with_lp):
                    if r == 0:
                        return e.tensor_copy(Bv(RB, [(1, 1024)]),
                                             Bv(WDA, [(1, 1024)]))
                    return e.tensor_tensor(Bv(RB, [(1, 1024)]),
                                           Bv(RB, [(1, 1024)]),
                                           Bv(WDA, [(1, 1024)]), ALU.add)
            op("v", t1f, (vI, 1))
            op("v", t2f, (vI, 1))
            w("v", vI, vI.n)
            op("v", t3f, (vI, 1))
            w("v", vI, vI.n)
            op("v", accf, (vsm, 1))

        for r in range(4):
            emit_scores_tc(4 * r)
            emit_scores_tc(4 * r + 1)
            if r >= 1:
                emit_pv_round(r - 1)
            emit_scores_tc(4 * r + 2)
            emit_scores_tc(4 * r + 3)
            if h + 1 < H:
                if r < 2:
                    emit_kv_group("K", h + 1, 2 * r)
                    emit_kv_evac("K", h + 1, 2 * r)
                    emit_kv_group("K", h + 1, 2 * r + 1)
                    emit_kv_evac("K", h + 1, 2 * r + 1)
                else:
                    emit_kv_group("V", h + 1, 2 * (r - 2))
                    emit_kv_evac("V", h + 1, 2 * (r - 2))
                    emit_kv_group("V", h + 1, 2 * (r - 2) + 1)
                    emit_kv_evac("V", h + 1, 2 * (r - 2) + 1)
            emit_tree_round(r)
        emit_pv_round(3)
        if h + 2 < H:
            emit_wkv(h + 2)
        if h + 1 < H:
            emit_vt(h + 1)
            emit_vt_copies(h + 1)
        w("t", vsm, 4 * h + 4)
        if h + 1 < H:
            w("t", kv_evac[("V", h + 1, 3)])
        for half in range(2):
            ride = (tRB, 1) if half == 1 else None
            op("t", lambda e, half=half: e.matmul(
                Pv(2048 + half * 512, [(1, 512)]),
                Bv(ONESC, [(1, 128)]),
                Bv(RB + half * 512, [(1, 512)]),
                start=True, stop=True), ride)
        w("v", tRB, h + 1)
        if h >= 1:
            w("v", vO, h)
        op("v", lambda e: e.reciprocal(
            Fv(T1, [(1, 1024)]), Pv(2048, [(1, 1024)])), (vRc, 1))
        w("v", tRd, 4 * h + 4)
        w("v", vRc, vRc.n)
        op("v", lambda e, h=h: e.tensor_tensor(
            Bv(OFM + h * 1024, [(1, 1024)]),
            Pv(3072, [(1, 1024)]),
            Fv(T1, [(1, 1024)]), ALU.mult), (vO, 1))

    marks["C"] = {e: len(l) for e, l in ops.items()}
    w("g", tKV, 128)
    for gchunk in range(4):
        op("g", lambda e, gchunk=gchunk: e.dma_start(
            Bv(gchunk * 4 * 2048, [(2048, 4), (1, 2048)]),
            bass.AP(wo.tensor, gchunk * 4 * 128 * D,
                    [[D, 128], [128 * D, 4], [1, D]])), (dWO[gchunk], 16))
    marks["Dwo"] = {e: len(l) for e, l in ops.items()}

    for qc in range(NQC):
        slot = qc % 2
        if qc < 2:
            w("y", vXA, 16)
        else:
            w("y", vD, qc - 1)
        op("y", lambda e, qc=qc, slot=slot: e.dma_start(
            Fv(XS0 + slot * 2048, [(1, 2048)]),
            xb[qc * 128:(qc + 1) * 128, :]), (dXO[slot], 16))
        pair = qc % 2
        if qc == 0:
            w("t", sE, 256)
            w("t", vO, 16)
        elif qc == 1:
            w("t", vRc, 16)
        else:
            w("t", vD, qc - 1)
        for gchunk in range(4):
            w("t", dWO[gchunk], 16)
            for kc in range(4 * gchunk, 4 * gchunk + 4):
                for j in range(4):
                    ride = (tD, 1) if (kc == NKC - 1 and j == 3) else None
                    op("t", lambda e, kc=kc, j=j, qc=qc, pair=pair: e.matmul(
                        Pv(pair * 2048 + j * 512, [(1, 512)]),
                        Bv(OFM + kc * 1024 + qc * 128, [(1, 128)]),
                        Bv(kc * 2048 + j * 512, [(1, 512)]),
                        start=(kc == 0), stop=(kc == NKC - 1)), ride)
        if qc == 0:
            marks["Datt"] = {e: len(l) for e, l in ops.items()}
        ps_v = (lambda pair=pair: Pv(pair * 2048, [(1, 2048)]))
        par, M, RS = ln_site(
            src_v=ps_v, src_s=ps_v,
            dump=lambda: Bv(PTR, [(1, 2048)]),
            vwaits=[(tD, qc + 1)], swaits=[(tD, qc + 1)], psum_src=True)
        if qc == 0:
            marks["Dln2"] = {e: len(l) for e, l in ops.items()}
        w("v", vI, vI.n)
        if qc >= 1:
            w("v", dSP[(qc - 1) % 2], 16 * ((qc - 1) // 2 + 1))
        op("v", lambda e, pair=pair, par=par, M=M, RS=RS: e.tensor_scalar(
            Fv(T1, [(1, 2048)]),
            Pv(pair * 2048, [(1, 2048)]),
            stat(par, M), stat(par, RS), ALU.subtract, ALU.mult), (vI, 1))
        if qc == 0:
            marks["Dt1"] = {e: len(l) for e, l in ops.items()}
        w("v", dXO[slot], 16 * (qc // 2 + 1))
        w("v", vI, vI.n)
        op("v", lambda e, slot=slot: e.tensor_tensor(
            Fv(T1, [(1, 2048)]),
            Fv(T1, [(1, 2048)]),
            Fv(XS0 + slot * 2048, [(1, 2048)]), ALU.add), (vI, 1))
        if qc == 0:
            marks["Dx2"] = {e: len(l) for e, l in ops.items()}
        t1_v = (lambda: Fv(T1, [(1, 2048)]))
        w("s", vI, vI.n)
        par3, M3, RS3 = ln_site(
            src_v=t1_v, src_s=t1_v,
            dump=lambda: Bv(PTR, [(1, 2048)]),
            vwaits=[(vI, vI.n)], swaits=[(vI, vI.n)])
        if qc == 0:
            marks["Dln3"] = {e: len(l) for e, l in ops.items()}
        w("v", vI, vI.n)
        op("v", lambda e, qc=qc, par3=par3, M3=M3, RS3=RS3: e.tensor_scalar(
            Bv(QF + qc * 2048, [(1, 2048)]),
            Fv(T1, [(1, 2048)]),
            stat(par3, M3), stat(par3, RS3), ALU.subtract, ALU.mult), (vD, 1))
        w("g", vD, qc + 1)
        op("g", lambda e, qc=qc: e.dma_start(
            x2sa[qc * 128:(qc + 1) * 128, :],
            Fv(T1, [(1, 2048)])), (dSP[qc % 2], 16))
        marks["D%d" % qc] = {e: len(l) for e, l in ops.items()}

    marks["D"] = {e: len(l) for e, l in ops.items()}
    for qc in range(NQC):
        if qc == 0:
            w("t", vD, 7)
        elif qc == 1:
            w("t", vD, 8)
        else:
            w("t", vA3, qc - 1)
        w("t", vD, min(qc + 1, 8))
        pair = qc % 2
        for kc in range(NKC):
            ride = (tT3, 1) if kc == NKC - 1 else None
            op("t", lambda e, qc=qc, kc=kc, pair=pair: e.matmul(
                Pv(pair * 2048 + kc * 128, [(1, 128)]),
                Bv(QF + qc * 2048 + kc * 128, [(1, 128)]),
                Bv(IDENT, [(1, 128)]),
                start=True, stop=True), ride)
        w("v", tT3, qc + 1)
        op("v", lambda e, qc=qc, pair=pair: e.tensor_copy(
            Bv(OFM + qc * 128, [(1024, 16), (1, 128)]),
            Pv(pair * 2048, [(128, 16), (1, 128)])), (vA3, 1))

    for hc in range(NHC):
        m_g = 48 + 2 * hc
        slot_g, dwv_g = wdma(wg[hc, :, :], m_g, consumed_at.get(m_g - 3))
        m_u = 49 + 2 * hc
        slot_u, dwv_u = wdma(wu[hc, :, :], m_u, consumed_at.get(m_u - 3))

        gb = 4 * (hc % 2)
        ub = 4 * (hc % 2) + 2
        w("t", dW[slot_g], dwv_g)
        if hc == 0:
            w("t", vA3, 8)
        if hc >= 2:
            w("t", sSil, hc - 1)
        for kc in range(NKC):
            for half in range(2):
                ride = (tE, 1) if (kc == NKC - 1 and half == 1) else None
                op("t", lambda e, slot=slot_g, kc=kc, half=half, gb=gb: e.matmul(
                    Pv(gb * 512 + half * 512, [(1, 512)]),
                    Bv(W3 + slot * 2048 + kc * 128, [(1, 128)]),
                    Bv(OFM + kc * 1024 + half * 512, [(1, 512)]),
                    start=(kc == 0), stop=(kc == NKC - 1)), ride)
        consumed_at[m_g] = (tE, tE.n)
        w("t", dW[slot_u], dwv_u)
        if hc >= 2:
            w("t", vGa, hc - 1)
        for kc in range(NKC):
            for half in range(2):
                ride = (tE, 1) if (kc == NKC - 1 and half == 1) else None
                op("t", lambda e, slot=slot_u, kc=kc, half=half, ub=ub: e.matmul(
                    Pv(ub * 512 + half * 512, [(1, 512)]),
                    Bv(W3 + slot * 2048 + kc * 128, [(1, 128)]),
                    Bv(OFM + kc * 1024 + half * 512, [(1, 512)]),
                    start=(kc == 0), stop=(kc == NKC - 1)), ride)
        consumed_at[m_u] = (tE, tE.n)
        w("s", tE, 2 * hc + 1)
        if hc == 0:
            w("s", dSP[0], 16 * 4)
            w("s", dSP[1], 16 * 4)
        w("s", vGa, hc)
        op("s", lambda e, gb=gb: e.activation(
            Fv(T1, [(1, 1024)]),
            Pv(gb * 512, [(1, 1024)]), AF.Silu), (sSil, 1))
        w("v", tE, 2 * hc + 2)
        w("v", sSil, hc + 1)
        op("v", lambda e, hc=hc, ub=ub: e.tensor_tensor(
            Bv(GMAP[hc], [(1, 1024)]),
            Fv(T1, [(1, 1024)]),
            Pv(ub * 512, [(1, 1024)]), ALU.mult), (vGa, 1))

    marks["E"] = {e: len(l) for e, l in ops.items()}
    for j in range(4):
        for qd in range(16):
            Q = 16 * j + qd
            base = WDA if Q % 2 == 0 else WDB
            if Q >= 2:
                w("g", tF, Q - 1)
            op("g", lambda e, j=j, qd=qd, base=base: e.dma_start(
                Bv(base, [(512, 4), (1, 512)]),
                bass.AP(wd.tensor, (j * NHC + 4 * qd) * 128 * 512,
                        [[512, 128], [128 * 512, 4], [1, 512]])),
               (dWD[Q % 2], 16))
        for qd in range(16):
            Q = 16 * j + qd
            base = WDA if Q % 2 == 0 else WDB
            w("t", dWD[Q % 2], 16 * (Q // 2 + 1))
            if Q == 0:
                w("t", sSil, 64)
                w("t", vGa, 64)
            if qd == 0 and j >= 1:
                w("t", sFe, 4 * j)
                w("t", vFe, 4 * j)
            for i in range(4):
                hc = 4 * qd + i
                for qc in range(NQC):
                    ride = (tF, 1) if (i == 3 and qc == NQC - 1) else None
                    op("t", lambda e, hc=hc, qc=qc, i=i, base=base, j=j: e.matmul(
                        Pv(qc * 512, [(1, 512)]),
                        Bv(GMAP[hc] + qc * 128, [(1, 128)]),
                        Bv(base + i * 512, [(1, 512)]),
                        start=(hc == 0), stop=(hc == NHC - 1)), ride)
        w("s", tF, 16 * (j + 1))
        for qc in range(4):
            op("s", lambda e, qc=qc, j=j: e.activation(
                Bv(OFM + qc * 2048 + j * 512, [(1, 512)]),
                Pv(qc * 512, [(1, 512)]), AF.Copy), (sFe, 1))
        w("s", sFe, sFe.n)
        for qc in range(4):
            w("s", sFs, sFs.n)
            op("s", lambda e, qc=qc, j=j: e.activation(
                Fv(XS0, [(1, 512)]),
                Bv(OFM + qc * 2048 + j * 512, [(1, 512)]),
                AF.Square, accum_out=Fv(ST + 96 + qc * 4 + j, [(1, 1)])), (sFs, 1))
            w("s", sFs, sFs.n)
            op("s", lambda e, qc=qc, j=j: e.activation(
                Fv(XS0, [(1, 512)]),
                Bv(OFM + qc * 2048 + j * 512, [(1, 512)]),
                AF.Identity, accum_out=Fv(ST + 64 + qc * 4 + j, [(1, 1)])), (sFs, 1))
        w("v", tF, 16 * (j + 1))
        for qc in range(4, 8):
            op("v", lambda e, qc=qc, j=j: e.tensor_copy(
                Bv(OFM + qc * 2048 + j * 512, [(1, 512)]),
                Pv(qc * 512, [(1, 512)])), (vFe, 1))
        w("v", vFe, vFe.n)
        for qc in range(4, 8):
            op("v", lambda e, qc=qc, j=j: e.reduce_sum(
                Fv(ST + 64 + qc * 4 + j, [(1, 1)]),
                Bv(OFM + qc * 2048 + j * 512, [(1, 512)]), axis=AX.X), (vFs, 1))
            w("v", vFs, vFs.n)
            def sq(e, qc=qc, j=j):
                return e.scalar_tensor_tensor(
                    Fv(XS1, [(1, 512)]),
                    Bv(OFM + qc * 2048 + j * 512, [(1, 512)]),
                    1.0,
                    Bv(OFM + qc * 2048 + j * 512, [(1, 512)]),
                    ALU.mult, ALU.mult,
                    accum_out=Fv(ST + 96 + qc * 4 + j, [(1, 1)]))
            op("v", sq, (vFs, 1))

    marks["F"] = {e: len(l) for e, l in ops.items()}
    w("g", dSP[0], 16 * 4)
    w("g", dSP[1], 16 * 4)
    op("g", lambda e: e.dma_start(out[:, :], x2sa[:, :]), (dOC, 16))

    for qc in range(NQC):
        par = qc % 4
        SM, SQ2, M, M2, VE, SD, RS, MR = 0, 1, 2, 3, 4, 5, 6, 7
        if qc == 0:
            w("v", sFs, 32)
            w("v", vFs, 32)
            w("v", sFe, 16)
            w("v", vFe, 16)
            w("v", tF, 64)
            w("s", sFs, 32)
            w("s", vFs, 32)
            w("s", sFe, 16)
            w("s", vFe, 16)
            w("s", tF, 64)
        w("v", vI, vI.n)
        op("v", lambda e, par=par, qc=qc: e.tensor_tensor(
            stat(par, SM), Fv(ST + 64 + qc * 4, [(1, 1)]),
            Fv(ST + 64 + qc * 4 + 1, [(1, 1)]), ALU.add), (vI, 1))
        op("v", lambda e, par=par, qc=qc: e.tensor_tensor(
            stat(par, M2), Fv(ST + 64 + qc * 4 + 2, [(1, 1)]),
            Fv(ST + 64 + qc * 4 + 3, [(1, 1)]), ALU.add), (vI, 1))
        w("v", vI, vI.n)
        op("v", lambda e, par=par: e.tensor_tensor(
            stat(par, SM), stat(par, SM), stat(par, M2), ALU.add), (vI, 1))
        op("v", lambda e, par=par, qc=qc: e.tensor_tensor(
            stat(par, SQ2), Fv(ST + 96 + qc * 4, [(1, 1)]),
            Fv(ST + 96 + qc * 4 + 1, [(1, 1)]), ALU.add), (vI, 1))
        w("v", vI, vI.n)
        op("v", lambda e, par=par, qc=qc: e.tensor_tensor(
            stat(par, M2), Fv(ST + 96 + qc * 4 + 2, [(1, 1)]),
            Fv(ST + 96 + qc * 4 + 3, [(1, 1)]), ALU.add), (vI, 1))
        w("v", vI, vI.n)
        op("v", lambda e, par=par: e.tensor_tensor(
            stat(par, SQ2), stat(par, SQ2), stat(par, M2), ALU.add), (vI, 1))
        w("v", vI, vI.n)
        op("v", lambda e, par=par: e.tensor_scalar(
            stat(par, M), stat(par, SM), 1.0 / D, None, ALU.mult), (vI, 1))
        w("v", vI, vI.n)
        op("v", lambda e, par=par: e.tensor_scalar(
            stat(par, M2), stat(par, M), stat(par, M), None, ALU.mult), (vI, 1))
        w("v", vI, vI.n)
        op("v", lambda e, par=par: e.scalar_tensor_tensor(
            stat(par, VE), stat(par, SQ2), 1.0 / D, stat(par, M2),
            ALU.mult, ALU.subtract), (vI, 1))
        w("s", vI, vI.n)
        op("s", lambda e, par=par: e.activation(
            stat(par, SD), stat(par, VE), AF.Sqrt,
            bias=Fv(ST + 40, [(1, 1)])), (sSd, 1))
        w("v", sSd, sSd.n)
        op("v", lambda e, par=par: e.reciprocal(stat(par, RS), stat(par, SD)), (vI, 1))
        w("v", vI, vI.n)
        op("v", lambda e, par=par: e.tensor_scalar(
            stat(par, MR), stat(par, M),
            stat(par, RS), -1.0, ALU.mult, ALU.mult), (vI, 1))
        slot = qc % 3
        w("s", vI, vI.n)
        if qc >= 3:
            w("s", dAC[slot], 16 * (qc // 3))
        op("s", lambda e, qc=qc, slot=slot, par=par: e.activation(
            Fv(slot * 2048, [(1, 2048)]),
            Bv(OFM + qc * 2048, [(1, 2048)]),
            AF.Identity, bias=stat(par, MR), scale=stat(par, RS)), (vG, 1))
        w("g", dOC, 16)
        w("g", vG, qc + 1)
        op("g", lambda e, qc=qc, slot=slot: e.dma_start(
            out[qc * 128:(qc + 1) * 128, :],
            Fv(slot * 2048, [(1, 2048)]),
            accum_op=ALU.add), (dAC[slot], 16))

    w("g", dAC[0], 16 * 3)
    w("g", dAC[1], 16 * 3)
    w("g", dAC[2], 16 * 2)
    w("g", dOC, 16)

    marks["G"] = {e: len(l) for e, l in ops.items()}
    assert tKV.n == 128 and tS.n == 256 and tRd.n == 64
    assert tQ.n == 16 and sQe.n == 16 and sKV.n == 128 and vO.n == 16
    assert tD.n == 8 and vD.n == 8 and tE.n == 128 and vGa.n == 64
    assert tF.n == 64 and sFe.n == 16 and vFe.n == 16 and vG.n == 8
    assert sFs.n == 32 and vFs.n == 32
    assert tTR.n == 16 and vA.n == 16 and vXA.n == 16 and tT3.n == 8 and vA3.n == 8
    assert tVt.n == 32 and vVt.n == 32 and tRB.n == 16 and vRc.n == 16 and sE.n == 256
    assert vsm.n == 64

    cut = marks[_STOP_AFTER]
    with nc.Block() as block:
        @block.tensor
        def _(eng):
            for f in ops["t"][:cut["t"]]:
                f(eng)

        @block.vector
        def _(eng):
            for f in ops["v"][:cut["v"]]:
                f(eng)

        @block.scalar
        def _(eng):
            for f in ops["s"][:cut["s"]]:
                f(eng)

        @block.gpsimd
        def _(eng):
            for f in ops["g"][:cut["g"]]:
                f(eng)

        @block.sync
        def _(eng):
            for f in ops["y"][:cut["y"]]:
                f(eng)

    es.close()
    return nc


_NC_CACHE = [None]


def _host_layouts(w_q, w_k, w_v, w_o, w_gate, w_up, w_down):
    def qkv(wm):
        return np.ascontiguousarray(
            wm.reshape(H, 128, NKC, 128).transpose(0, 3, 2, 1)).reshape(H, 128, D)

    wq_t = qkv(w_q)
    wk_t = qkv(w_k)
    wv_t = qkv(w_v)
    wo_t = np.ascontiguousarray(
        w_o.reshape(D, NKC, 128).transpose(1, 2, 0))
    def gu(wm):
        return np.ascontiguousarray(
            wm.reshape(NHC, 128, NKC, 128).transpose(0, 3, 2, 1)).reshape(NHC, 128, D)
    wg_t = gu(w_gate)
    wu_t = gu(w_up)
    wd_t = np.ascontiguousarray(
        w_down.reshape(4, 512, NHC, 128).transpose(0, 2, 3, 1))
    return wq_t, wk_t, wv_t, wo_t, wg_t, wu_t, wd_t


def kernel(L0_attn_x, L0_attn_w_q, L0_attn_w_k, L0_attn_w_v, L0_attn_w_o,
           L0_ffn_w_gate, L0_ffn_w_up, L0_ffn_w_down):
    x = np.asarray(L0_attn_x, dtype=np.float32)
    wq_t, wk_t, wv_t, wo_t, wg_t, wu_t, wd_t = _host_layouts(
        np.asarray(L0_attn_w_q, np.float32), np.asarray(L0_attn_w_k, np.float32),
        np.asarray(L0_attn_w_v, np.float32), np.asarray(L0_attn_w_o, np.float32),
        np.asarray(L0_ffn_w_gate, np.float32), np.asarray(L0_ffn_w_up, np.float32),
        np.asarray(L0_ffn_w_down, np.float32))
    idn = np.eye(128, dtype=np.float32)
    onc = np.ones((128, 128), np.float32)
    onr = np.ones((1, 128), np.float32)
    cstm = np.tile(np.array([[EPS, -SH]], np.float32), (128, 1))

    if _NC_CACHE[0] is None:
        _NC_CACHE[0] = build_nc()
    nc = _NC_CACHE[0]

    shared = {"wq": wq_t, "wk": wk_t, "wv": wv_t, "wo": wo_t,
              "wg": wg_t, "wu": wu_t, "wd": wd_t,
              "idn": idn, "onc": onc, "onr": onr, "cst": cstm}
    in_maps = []
    for core in range(8):
        b, half = core // 2, core % 2
        own = x[b, half * NQ:(half + 1) * NQ, :]
        oth = x[b, (1 - half) * NQ:(2 - half) * NQ, :]
        xbp = np.ascontiguousarray(np.concatenate([own, oth], axis=0))
        m = dict(shared)
        m["xb"] = xbp
        in_maps.append(m)

    res = run_bass_kernel_spmd(nc, in_maps, core_ids=list(range(8)))
    full = np.empty((4, S, D), np.float32)
    for core in range(8):
        b, half = core // 2, core % 2
        full[b, half * NQ:(half + 1) * NQ, :] = res.results[core]["out"]
    return full


# version 14 (final)
# speedup vs baseline: 1.0464x; 1.0374x over previous
import numpy as np
import concourse.bass as bass
import concourse.mybir as mybir
from concourse.bass_utils import run_bass_kernel_spmd

F32 = mybir.dt.float32
BF16 = mybir.dt.bfloat16
AF = mybir.ActivationFunctionType
ALU = mybir.AluOpType
AX = mybir.AxisListType

D = 2048
NKC = 16
H = 16
S = 2048
NQ = 1024
NQC = 8
NTC = 16
NHC = 64
EPS = 1e-5
QK = float(np.float32(1.0) / np.float32(np.sqrt(128.0)))
SH = 3.0

LN1 = 0
QF = 32768
PTR = 49152
KSB = 53248
VFM = 57344
VTM = 59392
W3 = 61440
WDA = 67584
IDENT = 69632
ONESC = 69760
ONESR = 69888
RB = 70016
RECIP = 71040
OFM = 72064
TAIL = 88448
ABF_N = 92544

XS0 = 0
XS1 = 2048
T1 = 4096
ST = 6144
AF32_N = 6656

GMAP = {}
for _hc in range(32):
    GMAP[_hc] = 1024 * _hc
for _hc in range(32, 48):
    GMAP[_hc] = QF + 1024 * (_hc - 32)
GMAP[48] = 51200
GMAP[49] = 52224
for _hc in range(50, 54):
    GMAP[_hc] = KSB + 1024 * (_hc - 50)
GMAP[54] = VFM
GMAP[55] = VFM + 1024
GMAP[56] = VTM
GMAP[57] = VTM + 1024
GMAP[58] = RB
GMAP[59] = RECIP
for _hc in range(60, 64):
    GMAP[_hc] = TAIL + 1024 * (_hc - 60)
WDB = PTR


_STOP_AFTER = "G"


class Ctr:
    __slots__ = ("s", "n")

    def __init__(self, s):
        self.s = s
        self.n = 0


def build_nc():
    nc = bass.Bass("TRN2", target_bir_lowering=False)
    xb = nc.dram_tensor("xb", [S, D], F32, kind="ExternalInput").ap()
    wq = nc.dram_tensor("wq", [H, 128, D], F32, kind="ExternalInput").ap()
    wk = nc.dram_tensor("wk", [H, 128, D], F32, kind="ExternalInput").ap()
    wv = nc.dram_tensor("wv", [H, 128, D], F32, kind="ExternalInput").ap()
    wo = nc.dram_tensor("wo", [NKC, 128, D], F32, kind="ExternalInput").ap()
    wg = nc.dram_tensor("wg", [NHC, 128, D], F32, kind="ExternalInput").ap()
    wu = nc.dram_tensor("wu", [NHC, 128, D], F32, kind="ExternalInput").ap()
    wd = nc.dram_tensor("wd", [4, NHC, 128, 512], F32, kind="ExternalInput").ap()
    idn = nc.dram_tensor("idn", [128, 128], F32, kind="ExternalInput").ap()
    onc = nc.dram_tensor("onc", [128, 128], F32, kind="ExternalInput").ap()
    onr = nc.dram_tensor("onr", [1, 128], F32, kind="ExternalInput").ap()
    cst = nc.dram_tensor("cst", [128, 2], F32, kind="ExternalInput").ap()
    x2s = nc.dram_tensor("x2s", [NQ, D], F32)
    out = nc.dram_tensor("out", [NQ, D], F32, kind="ExternalOutput").ap()
    x2sa = x2s.ap()

    import contextlib
    es = contextlib.ExitStack()
    ABF = es.enter_context(nc.sbuf_tensor("abf", [128, ABF_N], BF16))
    AFP = es.enter_context(nc.sbuf_tensor("afp", [128, AF32_N], F32))
    PS = es.enter_context(nc.psum_tensor("ps", [128, 4096], F32))

    def Bv(off, dims, parts=128):
        return bass.AP(ABF, off, [[ABF_N, parts]] + [list(d) for d in dims])

    def Fv(off, dims, parts=128):
        return bass.AP(AFP, off, [[AF32_N, parts]] + [list(d) for d in dims])

    def Pv(off, dims, parts=128):
        return bass.AP(PS, off, [[4096, parts]] + [list(d) for d in dims])

    def stat(par, k):
        return Fv(ST + par * 8 + k, [(1, 1)])

    sems = {}

    def C(name):
        c = Ctr(es.enter_context(nc.semaphore(name)))
        sems[name] = c
        return c

    dM = C("dM")
    dW = [C("dW0"), C("dW1"), C("dW2")]
    dWO = [C("dWO0"), C("dWO1"), C("dWO2"), C("dWO3")]
    dWD = [C("dWDa"), C("dWDb")]
    dOC = C("dOC")
    dAC = [C("dAC0"), C("dAC1"), C("dAC2")]
    dXA = [C("dXA0"), C("dXA1"), C("dXA2")]
    dXO = [C("dXO0"), C("dXO1")]
    dSP = [C("dSP0"), C("dSP1")]
    tTR = C("tTR")
    tQ = C("tQ")
    tKV = C("tKV")
    tVt = C("tVt")
    tS = C("tS")
    tRd = C("tRd")
    tRB = C("tRB")
    tD = C("tD")
    tT3 = C("tT3")
    tE = C("tE")
    tF = C("tF")
    sI = C("sI")
    sSq = C("sSq")
    sSd = C("sSd")
    sQe = C("sQe")
    sKV = C("sKV")
    sE = C("sE")
    sRB = C("sRB")
    sSm = C("sSm")
    sSil = C("sSil")
    sFe = C("sFe")
    vI = C("vI")
    vXA = C("vXA")
    vA = C("vA")
    vVt = C("vVt")
    vRc = C("vRc")
    vO = C("vO")
    vD = C("vD")
    vA3 = C("vA3")
    vGa = C("vGa")
    vFe = C("vFe")
    vG = C("vG")
    sFs = C("sFs")
    vFs = C("vFs")

    ops = {"t": [], "v": [], "s": [], "g": [], "y": []}
    marks = {}

    def op(e, f, ride=None):
        pair = None
        if ride is not None:
            c, k = ride
            c.n += k
            pair = (c.s, k)
        if pair is None:
            ops[e].append(f)
        else:
            def g(eng, f=f, pair=pair):
                f(eng).then_inc(*pair)
            ops[e].append(g)

    def w(e, c, v=None):
        if v is None:
            c, v = c
        ops[e].append(lambda eng, s=c.s, v=v: eng.wait_ge(s, v))

    site_par = [0]

    def ln_site(src_v, src_s, dump, vwaits, swaits, psum_src=False):
        par = site_par[0] % 4
        site_par[0] += 1
        SUM, M, M2, VE, SD, RS, SQ = range(7)
        for c, v in vwaits:
            w("v", c, v)
        op("v", lambda e: e.reduce_sum(stat(par, SUM), src_v(), axis=AX.X), (vI, 1))
        for c, v in swaits:
            w("s", c, v)
        if psum_src:
            w("s", vI, vI.n)
        op("s", lambda e: e.activation(dump(), src_s(), AF.Square,
                                       accum_out=stat(par, SQ)), (sSq, 1))
        w("v", vI, vI.n)
        op("v", lambda e: e.tensor_scalar(stat(par, M), stat(par, SUM),
                                          1.0 / D, None, ALU.mult), (vI, 1))
        w("v", vI, vI.n)
        op("v", lambda e: e.tensor_scalar(stat(par, M2), stat(par, M),
                                          stat(par, M), None, ALU.mult), (vI, 1))
        w("v", sSq, sSq.n)
        w("v", vI, vI.n)
        op("v", lambda e: e.scalar_tensor_tensor(stat(par, VE), stat(par, SQ),
                                                 1.0 / D, stat(par, M2),
                                                 ALU.mult, ALU.subtract), (vI, 1))
        w("s", vI, vI.n)
        op("s", lambda e: e.activation(stat(par, SD), stat(par, VE), AF.Sqrt,
                                       bias=Fv(ST + 40, [(1, 1)])), (sSd, 1))
        w("v", sSd, sSd.n)
        op("v", lambda e: e.reciprocal(stat(par, RS), stat(par, SD)), (vI, 1))
        return par, M, RS

    op("g", lambda e: e.dma_start(Bv(IDENT, [(1, 128)]), idn[:, :]), (dM, 16))
    op("g", lambda e: e.dma_start(Bv(ONESC, [(1, 128)]), onc[:, :]), (dM, 16))
    op("g", lambda e: e.dma_start(Bv(ONESR, [(1, 128)], parts=1), onr[:, :]), (dM, 16))
    op("g", lambda e: e.dma_start(Fv(ST + 40, [(1, 2)]), cst[:, :],
                                  allow_slow_non_contiguous=True), (dM, 16))
    w("s", dM, 64)

    def a_stage1(tc):
        slot = tc % 3
        if tc >= 3:
            w("y", vXA, tc - 2)
        op("y", lambda e, slot=slot, tc=tc: e.dma_start(
            Fv(XS0 + slot * 2048, [(1, 2048)]),
            xb[tc * 128:(tc + 1) * 128, :]), (dXA[slot], 16))
        xs_v = (lambda slot=slot: Fv(XS0 + slot * 2048, [(1, 2048)]))
        par, M, RS = ln_site(
            src_v=xs_v, src_s=xs_v,
            dump=lambda: Bv(KSB, [(1, 2048)]),
            vwaits=[(dXA[slot], 16 * (tc // 3 + 1))],
            swaits=[(dXA[slot], 16 * (tc // 3 + 1))])
        w("v", vI, vI.n)
        op("v", lambda e, par=par, M=M, RS=RS: e.tensor_scalar(
            stat(par, 7), stat(par, M),
            stat(par, RS), -1.0, ALU.mult, ALU.mult), (vI, 1))
        w("s", vI, vI.n)
        if tc >= 2:
            w("s", tTR, tc - 1)
        op("s", lambda e, slot=slot, par=par, RS=RS: e.activation(
            Bv(OFM + (tc % 2) * 2048, [(1, 2048)]),
            Fv(XS0 + slot * 2048, [(1, 2048)]),
            AF.Identity, bias=stat(par, 7), scale=stat(par, RS)), (vXA, 1))

    def a_stage2(tc):
        slot = tc % 2
        if tc == 0:
            w("t", dM, 64)
        w("t", vXA, tc + 1)
        if tc >= 1:
            w("t", vA, tc)
        for kc in range(NKC):
            ride = (tTR, 1) if kc == NKC - 1 else None
            op("t", lambda e, slot=slot, kc=kc: e.matmul(
                Pv(kc * 128, [(1, 128)]),
                Bv(OFM + slot * 2048 + kc * 128, [(1, 128)]),
                Bv(IDENT, [(1, 128)]),
                start=True, stop=True), ride)
        w("v", tTR, tc + 1)
        op("v", lambda e, tc=tc: e.tensor_copy(
            Bv(LN1 + tc * 128, [(2048, 16), (1, 128)]),
            Pv(0, [(128, 16), (1, 128)])), (vA, 1))

    a_stage1(0)
    a_stage1(1)
    for tc in range(2, 8):
        a_stage1(tc)
        a_stage2(tc - 2)
    a_stage2(6)
    a_stage2(7)

    marks["A"] = {e: len(l) for e, l in ops.items()}
    consumed_at = {}
    wslot_cnt = [0, 0, 0]

    def wdma(src_ap, item, cons_wait):
        slot = item % 3
        if cons_wait is not None:
            w("g", cons_wait[0], cons_wait[1])
        wslot_cnt[slot] += 1
        op("g", lambda e, slot=slot, src_ap=src_ap: e.dma_start(
            Bv(W3 + slot * 2048, [(1, 2048)]), src_ap), (dW[slot], 16))
        return slot, 16 * wslot_cnt[slot]

    for h in range(H):
        if h < 8:
            a_stage1(8 + h)
        if 2 <= h < 10:
            a_stage2(6 + h)
        cons = (tQ, h - 2) if h >= 3 else None
        slot, dwv = wdma(wq[h, :, :], h, cons)
        if h == 0:
            w("t", vA, 8)
        w("t", dW[slot], dwv)
        if h >= 2:
            w("t", sQe, h - 1)
        pair = h % 2
        for kc in range(NKC):
            for half in range(2):
                ride = (tQ, 1) if (kc == NKC - 1 and half == 1) else None
                op("t", lambda e, slot=slot, kc=kc, half=half, pair=pair: e.matmul(
                    Pv(2048 + pair * 1024 + half * 512, [(1, 512)]),
                    Bv(W3 + slot * 2048 + kc * 128, [(1, 128)]),
                    Bv(LN1 + kc * 2048 + half * 512, [(1, 512)]),
                    start=(kc == 0), stop=(kc == NKC - 1)), ride)
        consumed_at[h] = (tQ, tQ.n)
        w("s", tQ, h + 1)
        op("s", lambda e, h=h, pair=pair: e.activation(
            Bv(QF + h * 1024, [(1, 1024)]),
            Pv(2048 + pair * 1024, [(1, 1024)]), AF.Copy), (sQe, 1))

    marks["B"] = {e: len(l) for e, l in ops.items()}
    kv_evac = {}
    kv_mm = {}
    vt_mm = {}
    wk_slot = {}
    wv_slot = {}
    vsm = C("vSm")

    def emit_wkv(h):
        m_k = 16 + 2 * h
        cons_k = consumed_at.get(m_k - 3)
        wk_slot[h] = wdma(wk[h, :, :], m_k, cons_k)
        m_v = 17 + 2 * h
        cons_v = consumed_at.get(m_v - 3)
        wv_slot[h] = wdma(wv[h, :, :], m_v, cons_v)

    def emit_kv_group(kind, h, g):
        slot, dwv = wk_slot[h] if kind == "K" else wv_slot[h]
        if g == 0:
            w("t", dW[slot], dwv)
        if kind == "K":
            if h == 0:
                w("t", vA, 4 * (g + 1))
            if h == 0 and g < 2:
                w("t", sQe, 16)
            elif g < 2 and h >= 1:
                w("t", kv_evac[("V", h - 1, g + 2)])
            elif g >= 2:
                w("t", kv_evac[("K", h, g - 2)])
            if h >= 2 and g < 2:
                w("t", vRc, h - 1)
        else:
            w("t", kv_evac[("K", h, g + 2)] if g < 2 else kv_evac[("V", h, g - 2)])
        bank = 4 + (g % 2)
        for kc in range(NKC):
            ride = (tKV, 1) if kc == NKC - 1 else None
            op("t", lambda e, slot=slot, kc=kc, g=g, bank=bank: e.matmul(
                Pv(bank * 512, [(1, 512)]),
                Bv(W3 + slot * 2048 + kc * 128, [(1, 128)]),
                Bv(LN1 + kc * 2048 + g * 512, [(1, 512)]),
                start=(kc == 0), stop=(kc == NKC - 1)), ride)
        kv_mm[(kind, h, g)] = (tKV, tKV.n)
        if kind == "K" and g == 3:
            consumed_at[16 + 2 * h] = (tKV, tKV.n)
        if kind == "V" and g == 3:
            consumed_at[17 + 2 * h] = (tKV, tKV.n)

    def emit_kv_evac(kind, h, g):
        w("s", kv_mm[(kind, h, g)])
        if kind == "V" and g == 0 and h >= 1:
            w("s", vt_mm[(h - 1, 1)])
        bank = 4 + (g % 2)
        dst = (KSB + (h % 2) * 2048) if kind == "K" else VFM
        op("s", lambda e, g=g, dst=dst, bank=bank: e.activation(
            Bv(dst + g * 512, [(1, 512)]),
            Pv(bank * 512, [(1, 512)]), AF.Copy), (sKV, 1))
        kv_evac[(kind, h, g)] = (sKV, sKV.n)

    def emit_vt(h):
        for r in range(2):
            w("t", kv_evac[("V", h, 2 * r + 1)])
            if r == 0 and h == 0:
                w("t", vA, 16)
            if r == 0 and h >= 1:
                w("t", sE, 16 * h)
            for i in range(8):
                tcv = r * 8 + i
                ride = (tVt, 1) if i == 7 else None
                op("t", lambda e, tcv=tcv, r=r, i=i: e.matmul(
                    Pv(r * 1024 + i * 128, [(1, 128)]),
                    Bv(VFM + tcv * 128, [(1, 128)]),
                    Bv(IDENT, [(1, 128)]),
                    start=True, stop=True), ride)
            vt_mm[(h, r)] = (tVt, tVt.n)

    def emit_vt_copies(h):
        for r in range(2):
            w("v", vt_mm[(h, r)])
            if h >= 1 and r == 0:
                w("v", tRd, 4 * h)
            op("v", lambda e, r=r: e.tensor_copy(
                Bv(VTM + r * 1024, [(1, 1024)]),
                Pv(r * 1024, [(1, 1024)])), (vVt, 1))

    emit_wkv(0)
    for g in range(4):
        emit_kv_group("K", 0, g)
        emit_kv_evac("K", 0, g)
    for g in range(4):
        emit_kv_group("V", 0, g)
        emit_kv_evac("V", 0, g)
    emit_wkv(1)
    emit_vt(0)
    emit_vt_copies(0)

    for h in range(H):

        def emit_scores_tc(tc, h=h):
            pair = tc % 2
            if tc == 0:
                w("t", kv_evac[("K", h, 3)])
                w("t", vVt, 2 * h + 1)
            elif tc == 1:
                w("t", vVt, 2 * h + 2)
            else:
                w("t", sE, 16 * h + tc - 1)
            for half in range(2):
                ride = (tS, 1) if half == 1 else None
                op("t", lambda e, tc=tc, half=half, pair=pair, h=h: e.matmul(
                    Pv(pair * 1024 + half * 512, [(1, 512)]),
                    Bv(KSB + (h % 2) * 2048 + tc * 128, [(1, 128)]),
                    Bv(QF + h * 1024 + half * 512, [(1, 512)]),
                    start=True, stop=True), ride)
            r = tc // 4
            w("s", tS, 16 * h + tc + 1)
            w("s", tRd, 4 * h + r)
            w("s", vsm, 4 * h + r)
            op("s", lambda e, tc=tc, pair=pair: e.activation(
                Bv(PTR + (tc % 4) * 1024, [(1, 1024)]),
                Pv(pair * 1024, [(1, 1024)]),
                AF.Exp, bias=Fv(ST + 41, [(1, 1)]), scale=QK), (sE, 1))

        def emit_pv_round(r, h=h):
            w("t", sE, 16 * h + 4 * (r + 1))
            if r == 0:
                w("t", vO, h)
            for i in range(4):
                tcv = 4 * r + i
                for half in range(2):
                    ride = (tRd, 1) if (i == 3 and half == 1) else None
                    op("t", lambda e, tcv=tcv, i=i, half=half, r=r: e.matmul(
                        Pv(3072 + half * 512, [(1, 512)]),
                        Bv(VTM + tcv * 128, [(1, 128)]),
                        Bv(PTR + i * 1024 + half * 512, [(1, 512)]),
                        start=(r == 0 and i == 0), stop=(r == 3 and i == 3)), ride)

        def emit_tree_round(r, h=h):
            w("v", sE, 16 * h + 4 * (r + 1))
            with_lp = "bf16 softmax denominator tree (rvr impact <1e-5)"
            def t1f(e):
                with nc.allow_low_precision(with_lp):
                    return e.tensor_tensor(Bv(WDA, [(1, 1024)]),
                                           Bv(PTR, [(1, 1024)]),
                                           Bv(PTR + 1024, [(1, 1024)]), ALU.add)
            def t2f(e):
                with nc.allow_low_precision(with_lp):
                    return e.tensor_tensor(Bv(WDA + 1024, [(1, 1024)]),
                                           Bv(PTR + 2048, [(1, 1024)]),
                                           Bv(PTR + 3072, [(1, 1024)]), ALU.add)
            def t3f(e):
                with nc.allow_low_precision(with_lp):
                    return e.tensor_tensor(Bv(WDA, [(1, 1024)]),
                                           Bv(WDA, [(1, 1024)]),
                                           Bv(WDA + 1024, [(1, 1024)]), ALU.add)
            def accf(e, r=r):
                with nc.allow_low_precision(with_lp):
                    if r == 0:
                        return e.tensor_copy(Bv(RB, [(1, 1024)]),
                                             Bv(WDA, [(1, 1024)]))
                    return e.tensor_tensor(Bv(RB, [(1, 1024)]),
                                           Bv(RB, [(1, 1024)]),
                                           Bv(WDA, [(1, 1024)]), ALU.add)
            op("v", t1f, (vI, 1))
            op("v", t2f, (vI, 1))
            w("v", vI, vI.n)
            op("v", t3f, (vI, 1))
            w("v", vI, vI.n)
            op("v", accf, (vsm, 1))

        def emit_filler(h, idx):
            if h + 1 >= H:
                return
            kind = "K" if idx < 4 else "V"
            emit_kv_group(kind, h + 1, idx % 4)
            emit_kv_evac(kind, h + 1, idx % 4)

        for r in range(4):
            emit_scores_tc(4 * r)
            emit_scores_tc(4 * r + 1)
            if r >= 1:
                emit_pv_round(r - 1)
            emit_filler(h, 2 * r)
            emit_scores_tc(4 * r + 2)
            emit_scores_tc(4 * r + 3)
            emit_filler(h, 2 * r + 1)
            emit_tree_round(r)
        emit_pv_round(3)
        if h + 2 < H:
            emit_wkv(h + 2)
        if h + 1 < H:
            emit_vt(h + 1)
            emit_vt_copies(h + 1)
        w("t", vsm, 4 * h + 4)
        if h + 1 < H:
            w("t", kv_evac[("V", h + 1, 3)])
        for half in range(2):
            ride = (tRB, 1) if half == 1 else None
            op("t", lambda e, half=half: e.matmul(
                Pv(2048 + half * 512, [(1, 512)]),
                Bv(ONESC, [(1, 128)]),
                Bv(RB + half * 512, [(1, 512)]),
                start=True, stop=True), ride)
        w("v", tRB, h + 1)
        if h >= 1:
            w("v", vO, h)
        op("v", lambda e: e.reciprocal(
            Fv(T1, [(1, 1024)]), Pv(2048, [(1, 1024)])), (vRc, 1))
        w("v", tRd, 4 * h + 4)
        w("v", vRc, vRc.n)
        op("v", lambda e, h=h: e.tensor_tensor(
            Bv(OFM + h * 1024, [(1, 1024)]),
            Pv(3072, [(1, 1024)]),
            Fv(T1, [(1, 1024)]), ALU.mult), (vO, 1))

    marks["C"] = {e: len(l) for e, l in ops.items()}
    w("g", tKV, 128)
    for gchunk in range(4):
        op("g", lambda e, gchunk=gchunk: e.dma_start(
            Bv(gchunk * 4 * 2048, [(2048, 4), (1, 2048)]),
            bass.AP(wo.tensor, gchunk * 4 * 128 * D,
                    [[D, 128], [128 * D, 4], [1, D]])), (dWO[gchunk], 16))
    marks["Dwo"] = {e: len(l) for e, l in ops.items()}

    for qc in range(NQC):
        slot = qc % 2
        if qc < 2:
            w("y", vXA, 16)
        else:
            w("y", vD, qc - 1)
        op("y", lambda e, qc=qc, slot=slot: e.dma_start(
            Fv(XS0 + slot * 2048, [(1, 2048)]),
            xb[qc * 128:(qc + 1) * 128, :]), (dXO[slot], 16))
        pair = qc % 2
        if qc == 0:
            w("t", sE, 256)
            w("t", vO, 16)
        elif qc == 1:
            w("t", vRc, 16)
        else:
            w("t", vD, qc - 1)
        for gchunk in range(4):
            w("t", dWO[gchunk], 16)
            for kc in range(4 * gchunk, 4 * gchunk + 4):
                for j in range(4):
                    ride = (tD, 1) if (kc == NKC - 1 and j == 3) else None
                    op("t", lambda e, kc=kc, j=j, qc=qc, pair=pair: e.matmul(
                        Pv(pair * 2048 + j * 512, [(1, 512)]),
                        Bv(OFM + kc * 1024 + qc * 128, [(1, 128)]),
                        Bv(kc * 2048 + j * 512, [(1, 512)]),
                        start=(kc == 0), stop=(kc == NKC - 1)), ride)
        if qc == 0:
            marks["Datt"] = {e: len(l) for e, l in ops.items()}
        ps_v = (lambda pair=pair: Pv(pair * 2048, [(1, 2048)]))
        par, M, RS = ln_site(
            src_v=ps_v, src_s=ps_v,
            dump=lambda: Bv(PTR, [(1, 2048)]),
            vwaits=[(tD, qc + 1)], swaits=[(tD, qc + 1)], psum_src=True)
        if qc == 0:
            marks["Dln2"] = {e: len(l) for e, l in ops.items()}
        w("v", vI, vI.n)
        if qc >= 1:
            w("v", dSP[(qc - 1) % 2], 16 * ((qc - 1) // 2 + 1))
        op("v", lambda e, pair=pair, par=par, M=M, RS=RS: e.tensor_scalar(
            Fv(T1, [(1, 2048)]),
            Pv(pair * 2048, [(1, 2048)]),
            stat(par, M), stat(par, RS), ALU.subtract, ALU.mult), (vI, 1))
        if qc == 0:
            marks["Dt1"] = {e: len(l) for e, l in ops.items()}
        w("v", dXO[slot], 16 * (qc // 2 + 1))
        w("v", vI, vI.n)
        op("v", lambda e, slot=slot: e.tensor_tensor(
            Fv(T1, [(1, 2048)]),
            Fv(T1, [(1, 2048)]),
            Fv(XS0 + slot * 2048, [(1, 2048)]), ALU.add), (vI, 1))
        if qc == 0:
            marks["Dx2"] = {e: len(l) for e, l in ops.items()}
        t1_v = (lambda: Fv(T1, [(1, 2048)]))
        w("s", vI, vI.n)
        par3, M3, RS3 = ln_site(
            src_v=t1_v, src_s=t1_v,
            dump=lambda: Bv(PTR, [(1, 2048)]),
            vwaits=[(vI, vI.n)], swaits=[(vI, vI.n)])
        if qc == 0:
            marks["Dln3"] = {e: len(l) for e, l in ops.items()}
        w("v", vI, vI.n)
        op("v", lambda e, qc=qc, par3=par3, M3=M3, RS3=RS3: e.tensor_scalar(
            Bv(QF + qc * 2048, [(1, 2048)]),
            Fv(T1, [(1, 2048)]),
            stat(par3, M3), stat(par3, RS3), ALU.subtract, ALU.mult), (vD, 1))
        w("g", vD, qc + 1)
        op("g", lambda e, qc=qc: e.dma_start(
            x2sa[qc * 128:(qc + 1) * 128, :],
            Fv(T1, [(1, 2048)])), (dSP[qc % 2], 16))
        marks["D%d" % qc] = {e: len(l) for e, l in ops.items()}

    marks["D"] = {e: len(l) for e, l in ops.items()}
    for qc in range(NQC):
        if qc == 0:
            w("t", vD, 7)
        elif qc == 1:
            w("t", vD, 8)
        else:
            w("t", vA3, qc - 1)
        w("t", vD, min(qc + 1, 8))
        pair = qc % 2
        for kc in range(NKC):
            ride = (tT3, 1) if kc == NKC - 1 else None
            op("t", lambda e, qc=qc, kc=kc, pair=pair: e.matmul(
                Pv(pair * 2048 + kc * 128, [(1, 128)]),
                Bv(QF + qc * 2048 + kc * 128, [(1, 128)]),
                Bv(IDENT, [(1, 128)]),
                start=True, stop=True), ride)
        w("v", tT3, qc + 1)
        op("v", lambda e, qc=qc, pair=pair: e.tensor_copy(
            Bv(OFM + qc * 128, [(1024, 16), (1, 128)]),
            Pv(pair * 2048, [(128, 16), (1, 128)])), (vA3, 1))

    for hc in range(NHC):
        m_g = 48 + 2 * hc
        slot_g, dwv_g = wdma(wg[hc, :, :], m_g, consumed_at.get(m_g - 3))
        m_u = 49 + 2 * hc
        slot_u, dwv_u = wdma(wu[hc, :, :], m_u, consumed_at.get(m_u - 3))

        gb = 4 * (hc % 2)
        ub = 4 * (hc % 2) + 2
        w("t", dW[slot_g], dwv_g)
        if hc == 0:
            w("t", vA3, 8)
        if hc >= 2:
            w("t", sSil, hc - 1)
        for kc in range(NKC):
            for half in range(2):
                ride = (tE, 1) if (kc == NKC - 1 and half == 1) else None
                op("t", lambda e, slot=slot_g, kc=kc, half=half, gb=gb: e.matmul(
                    Pv(gb * 512 + half * 512, [(1, 512)]),
                    Bv(W3 + slot * 2048 + kc * 128, [(1, 128)]),
                    Bv(OFM + kc * 1024 + half * 512, [(1, 512)]),
                    start=(kc == 0), stop=(kc == NKC - 1)), ride)
        consumed_at[m_g] = (tE, tE.n)
        w("t", dW[slot_u], dwv_u)
        if hc >= 2:
            w("t", vGa, hc - 1)
        for kc in range(NKC):
            for half in range(2):
                ride = (tE, 1) if (kc == NKC - 1 and half == 1) else None
                op("t", lambda e, slot=slot_u, kc=kc, half=half, ub=ub: e.matmul(
                    Pv(ub * 512 + half * 512, [(1, 512)]),
                    Bv(W3 + slot * 2048 + kc * 128, [(1, 128)]),
                    Bv(OFM + kc * 1024 + half * 512, [(1, 512)]),
                    start=(kc == 0), stop=(kc == NKC - 1)), ride)
        consumed_at[m_u] = (tE, tE.n)
        w("s", tE, 2 * hc + 1)
        if hc == 0:
            w("s", dSP[0], 16 * 4)
            w("s", dSP[1], 16 * 4)
        w("s", vGa, hc)
        op("s", lambda e, gb=gb: e.activation(
            Fv(T1, [(1, 1024)]),
            Pv(gb * 512, [(1, 1024)]), AF.Silu), (sSil, 1))
        w("v", tE, 2 * hc + 2)
        w("v", sSil, hc + 1)
        op("v", lambda e, hc=hc, ub=ub: e.tensor_tensor(
            Bv(GMAP[hc], [(1, 1024)]),
            Fv(T1, [(1, 1024)]),
            Pv(ub * 512, [(1, 1024)]), ALU.mult), (vGa, 1))

    marks["E"] = {e: len(l) for e, l in ops.items()}
    for j in range(4):
        for qd in range(16):
            Q = 16 * j + qd
            base = WDA if Q % 2 == 0 else WDB
            if Q >= 2:
                w("g", tF, Q - 1)
            op("g", lambda e, j=j, qd=qd, base=base: e.dma_start(
                Bv(base, [(512, 4), (1, 512)]),
                bass.AP(wd.tensor, (j * NHC + 4 * qd) * 128 * 512,
                        [[512, 128], [128 * 512, 4], [1, 512]])),
               (dWD[Q % 2], 16))
        for qd in range(16):
            Q = 16 * j + qd
            base = WDA if Q % 2 == 0 else WDB
            w("t", dWD[Q % 2], 16 * (Q // 2 + 1))
            if Q == 0:
                w("t", sSil, 64)
                w("t", vGa, 64)
            if qd == 0 and j >= 1:
                w("t", sFe, 4 * j)
                w("t", vFe, 4 * j)
            for i in range(4):
                hc = 4 * qd + i
                for qc in range(NQC):
                    ride = (tF, 1) if (i == 3 and qc == NQC - 1) else None
                    op("t", lambda e, hc=hc, qc=qc, i=i, base=base, j=j: e.matmul(
                        Pv(qc * 512, [(1, 512)]),
                        Bv(GMAP[hc] + qc * 128, [(1, 128)]),
                        Bv(base + i * 512, [(1, 512)]),
                        start=(hc == 0), stop=(hc == NHC - 1)), ride)
        w("s", tF, 16 * (j + 1))
        for qc in range(4):
            op("s", lambda e, qc=qc, j=j: e.activation(
                Bv(OFM + qc * 2048 + j * 512, [(1, 512)]),
                Pv(qc * 512, [(1, 512)]), AF.Copy), (sFe, 1))
        w("s", sFe, sFe.n)
        for qc in range(4):
            w("s", sFs, sFs.n)
            op("s", lambda e, qc=qc, j=j: e.activation(
                Fv(XS0, [(1, 512)]),
                Bv(OFM + qc * 2048 + j * 512, [(1, 512)]),
                AF.Square, accum_out=Fv(ST + 96 + qc * 4 + j, [(1, 1)])), (sFs, 1))
            w("s", sFs, sFs.n)
            op("s", lambda e, qc=qc, j=j: e.activation(
                Fv(XS0, [(1, 512)]),
                Bv(OFM + qc * 2048 + j * 512, [(1, 512)]),
                AF.Identity, accum_out=Fv(ST + 64 + qc * 4 + j, [(1, 1)])), (sFs, 1))
        w("v", tF, 16 * (j + 1))
        for qc in range(4, 8):
            op("v", lambda e, qc=qc, j=j: e.tensor_copy(
                Bv(OFM + qc * 2048 + j * 512, [(1, 512)]),
                Pv(qc * 512, [(1, 512)])), (vFe, 1))
        w("v", vFe, vFe.n)
        for qc in range(4, 8):
            op("v", lambda e, qc=qc, j=j: e.reduce_sum(
                Fv(ST + 64 + qc * 4 + j, [(1, 1)]),
                Bv(OFM + qc * 2048 + j * 512, [(1, 512)]), axis=AX.X), (vFs, 1))
            w("v", vFs, vFs.n)
            def sq(e, qc=qc, j=j):
                return e.scalar_tensor_tensor(
                    Fv(XS1, [(1, 512)]),
                    Bv(OFM + qc * 2048 + j * 512, [(1, 512)]),
                    1.0,
                    Bv(OFM + qc * 2048 + j * 512, [(1, 512)]),
                    ALU.mult, ALU.mult,
                    accum_out=Fv(ST + 96 + qc * 4 + j, [(1, 1)]))
            op("v", sq, (vFs, 1))

    marks["F"] = {e: len(l) for e, l in ops.items()}
    w("g", dSP[0], 16 * 4)
    w("g", dSP[1], 16 * 4)
    op("g", lambda e: e.dma_start(out[:, :], x2sa[:, :]), (dOC, 16))

    for qc in range(NQC):
        par = qc % 4
        SM, SQ2, M, M2, VE, SD, RS, MR = 0, 1, 2, 3, 4, 5, 6, 7
        if qc == 0:
            w("v", sFs, 32)
            w("v", vFs, 32)
            w("v", sFe, 16)
            w("v", vFe, 16)
            w("v", tF, 64)
            w("s", sFs, 32)
            w("s", vFs, 32)
            w("s", sFe, 16)
            w("s", vFe, 16)
            w("s", tF, 64)
        w("v", vI, vI.n)
        op("v", lambda e, par=par, qc=qc: e.tensor_tensor(
            stat(par, SM), Fv(ST + 64 + qc * 4, [(1, 1)]),
            Fv(ST + 64 + qc * 4 + 1, [(1, 1)]), ALU.add), (vI, 1))
        op("v", lambda e, par=par, qc=qc: e.tensor_tensor(
            stat(par, M2), Fv(ST + 64 + qc * 4 + 2, [(1, 1)]),
            Fv(ST + 64 + qc * 4 + 3, [(1, 1)]), ALU.add), (vI, 1))
        w("v", vI, vI.n)
        op("v", lambda e, par=par: e.tensor_tensor(
            stat(par, SM), stat(par, SM), stat(par, M2), ALU.add), (vI, 1))
        op("v", lambda e, par=par, qc=qc: e.tensor_tensor(
            stat(par, SQ2), Fv(ST + 96 + qc * 4, [(1, 1)]),
            Fv(ST + 96 + qc * 4 + 1, [(1, 1)]), ALU.add), (vI, 1))
        w("v", vI, vI.n)
        op("v", lambda e, par=par, qc=qc: e.tensor_tensor(
            stat(par, M2), Fv(ST + 96 + qc * 4 + 2, [(1, 1)]),
            Fv(ST + 96 + qc * 4 + 3, [(1, 1)]), ALU.add), (vI, 1))
        w("v", vI, vI.n)
        op("v", lambda e, par=par: e.tensor_tensor(
            stat(par, SQ2), stat(par, SQ2), stat(par, M2), ALU.add), (vI, 1))
        w("v", vI, vI.n)
        op("v", lambda e, par=par: e.tensor_scalar(
            stat(par, M), stat(par, SM), 1.0 / D, None, ALU.mult), (vI, 1))
        w("v", vI, vI.n)
        op("v", lambda e, par=par: e.tensor_scalar(
            stat(par, M2), stat(par, M), stat(par, M), None, ALU.mult), (vI, 1))
        w("v", vI, vI.n)
        op("v", lambda e, par=par: e.scalar_tensor_tensor(
            stat(par, VE), stat(par, SQ2), 1.0 / D, stat(par, M2),
            ALU.mult, ALU.subtract), (vI, 1))
        w("s", vI, vI.n)
        op("s", lambda e, par=par: e.activation(
            stat(par, SD), stat(par, VE), AF.Sqrt,
            bias=Fv(ST + 40, [(1, 1)])), (sSd, 1))
        w("v", sSd, sSd.n)
        op("v", lambda e, par=par: e.reciprocal(stat(par, RS), stat(par, SD)), (vI, 1))
        w("v", vI, vI.n)
        op("v", lambda e, par=par: e.tensor_scalar(
            stat(par, MR), stat(par, M),
            stat(par, RS), -1.0, ALU.mult, ALU.mult), (vI, 1))
        slot = qc % 3
        w("s", vI, vI.n)
        if qc >= 3:
            w("s", dAC[slot], 16 * (qc // 3))
        op("s", lambda e, qc=qc, slot=slot, par=par: e.activation(
            Fv(slot * 2048, [(1, 2048)]),
            Bv(OFM + qc * 2048, [(1, 2048)]),
            AF.Identity, bias=stat(par, MR), scale=stat(par, RS)), (vG, 1))
        w("g", dOC, 16)
        w("g", vG, qc + 1)
        op("g", lambda e, qc=qc, slot=slot: e.dma_start(
            out[qc * 128:(qc + 1) * 128, :],
            Fv(slot * 2048, [(1, 2048)]),
            accum_op=ALU.add), (dAC[slot], 16))

    w("g", dAC[0], 16 * 3)
    w("g", dAC[1], 16 * 3)
    w("g", dAC[2], 16 * 2)
    w("g", dOC, 16)

    marks["G"] = {e: len(l) for e, l in ops.items()}
    assert tKV.n == 128 and tS.n == 256 and tRd.n == 64
    assert tQ.n == 16 and sQe.n == 16 and sKV.n == 128 and vO.n == 16
    assert tD.n == 8 and vD.n == 8 and tE.n == 128 and vGa.n == 64
    assert tF.n == 64 and sFe.n == 16 and vFe.n == 16 and vG.n == 8
    assert sFs.n == 32 and vFs.n == 32
    assert tTR.n == 16 and vA.n == 16 and vXA.n == 16 and tT3.n == 8 and vA3.n == 8
    assert tVt.n == 32 and vVt.n == 32 and tRB.n == 16 and vRc.n == 16 and sE.n == 256
    assert vsm.n == 64

    cut = marks[_STOP_AFTER]
    with nc.Block() as block:
        @block.tensor
        def _(eng):
            for f in ops["t"][:cut["t"]]:
                f(eng)

        @block.vector
        def _(eng):
            for f in ops["v"][:cut["v"]]:
                f(eng)

        @block.scalar
        def _(eng):
            for f in ops["s"][:cut["s"]]:
                f(eng)

        @block.gpsimd
        def _(eng):
            for f in ops["g"][:cut["g"]]:
                f(eng)

        @block.sync
        def _(eng):
            for f in ops["y"][:cut["y"]]:
                f(eng)

    es.close()
    return nc


_NC_CACHE = [None]


def _host_layouts(w_q, w_k, w_v, w_o, w_gate, w_up, w_down):
    def qkv(wm):
        return np.ascontiguousarray(
            wm.reshape(H, 128, NKC, 128).transpose(0, 3, 2, 1)).reshape(H, 128, D)

    wq_t = qkv(w_q)
    wk_t = qkv(w_k)
    wv_t = qkv(w_v)
    wo_t = np.ascontiguousarray(
        w_o.reshape(D, NKC, 128).transpose(1, 2, 0))
    def gu(wm):
        return np.ascontiguousarray(
            wm.reshape(NHC, 128, NKC, 128).transpose(0, 3, 2, 1)).reshape(NHC, 128, D)
    wg_t = gu(w_gate)
    wu_t = gu(w_up)
    wd_t = np.ascontiguousarray(
        w_down.reshape(4, 512, NHC, 128).transpose(0, 2, 3, 1))
    return wq_t, wk_t, wv_t, wo_t, wg_t, wu_t, wd_t


def kernel(L0_attn_x, L0_attn_w_q, L0_attn_w_k, L0_attn_w_v, L0_attn_w_o,
           L0_ffn_w_gate, L0_ffn_w_up, L0_ffn_w_down):
    x = np.asarray(L0_attn_x, dtype=np.float32)
    wq_t, wk_t, wv_t, wo_t, wg_t, wu_t, wd_t = _host_layouts(
        np.asarray(L0_attn_w_q, np.float32), np.asarray(L0_attn_w_k, np.float32),
        np.asarray(L0_attn_w_v, np.float32), np.asarray(L0_attn_w_o, np.float32),
        np.asarray(L0_ffn_w_gate, np.float32), np.asarray(L0_ffn_w_up, np.float32),
        np.asarray(L0_ffn_w_down, np.float32))
    idn = np.eye(128, dtype=np.float32)
    onc = np.ones((128, 128), np.float32)
    onr = np.ones((1, 128), np.float32)
    cstm = np.tile(np.array([[EPS, -SH]], np.float32), (128, 1))

    if _NC_CACHE[0] is None:
        _NC_CACHE[0] = build_nc()
    nc = _NC_CACHE[0]

    shared = {"wq": wq_t, "wk": wk_t, "wv": wv_t, "wo": wo_t,
              "wg": wg_t, "wu": wu_t, "wd": wd_t,
              "idn": idn, "onc": onc, "onr": onr, "cst": cstm}
    in_maps = []
    for core in range(8):
        b, half = core // 2, core % 2
        own = x[b, half * NQ:(half + 1) * NQ, :]
        oth = x[b, (1 - half) * NQ:(2 - half) * NQ, :]
        xbp = np.ascontiguousarray(np.concatenate([own, oth], axis=0))
        m = dict(shared)
        m["xb"] = xbp
        in_maps.append(m)

    res = run_bass_kernel_spmd(nc, in_maps, core_ids=list(range(8)))
    full = np.empty((4, S, D), np.float32)
    for core in range(8):
        b, half = core // 2, core % 2
        full[b, half * NQ:(half + 1) * NQ, :] = res.results[core]["out"]
    return full
